# version 13
# baseline (speedup 1.0000x reference)
import math
import numpy as np
from contextlib import ExitStack

import concourse.bass as bass
import concourse.mybir as mybir
from concourse.bass_utils import run_bass_kernel_spmd

F32 = mybir.dt.float32
BF16 = mybir.dt.bfloat16
AF = mybir.ActivationFunctionType
ALU = mybir.AluOpType

H = 2048
NK = 16
NH = 16
DK = 128
FF = 8192
NF = FF // 128
NE = 2
EPS = 1e-5
SH = 50.0
LNC = math.log(1.0 / math.sqrt(DK))
ACL = -135.0
RBAND = 8


def _preload_set(nc, eng, set_name):
    from concourse.hw_specs import get_activation_tables
    for idx, name in enumerate(get_activation_tables(nc.m.arch).keys()):
        if name == set_name:
            eng.add_instruction(mybir.InstLoadActFuncSet(
                name=nc.get_next_instruction_name(), act_func_set_id=idx, ins=[], outs=[]))
            return
    raise ValueError(set_name)


def build_program(T=2048, scratch_kind="Internal", dbg=False, stop_after=None):
    CH = T // 128
    NSL = T // 512
    PT = min(T, 1024)
    NP = T // PT
    NS4 = PT // 512
    assert NS4 == 2 and T % 1024 == 0

    nc = bass.Bass("TRN2", target_bir_lowering=False)

    def dram(name, shape, dt=F32, kind="ExternalInput"):
        return nc.dram_tensor(name, shape, dt, kind=kind).ap()

    x_fm = dram("x_fm", [NK, 128, T])
    wq = dram("wq", [NK, 128, H])
    wf = dram("wf", [NK, 128, H])
    wi = dram("wi", [NK, 128, H])
    wo = dram("wo", [NK, 128, H])
    wgt = dram("wgt", [NE, NF, 128, H])
    wut = dram("wut", [NE, NF, 128, H])
    wdt = dram("wdt", [NE, NF, 128, H])
    grep = dram("grep", [NE, NK, 128, 128])
    gwt = dram("gwt", [128, NH])
    cones = dram("cones", [128, 128])
    cident = dram("cident", [128, 128])
    ctri = dram("ctri", [128, 256])
    cbias = dram("cbias", [128, 8])

    q_s = dram("q_s", [T // 128, 128, H], BF16, kind=scratch_kind)
    g_s = dram("g_s", [T // 128, 128, H], F32, kind=scratch_kind)
    k_s = dram("k_s", [T // 128, 128, H], BF16, kind=scratch_kind)
    v_s = dram("v_s", [T // 128, 128, H], BF16, kind=scratch_kind)
    o_s = dram("o_s", [NH, 128, T], BF16, kind=scratch_kind)
    x1_s = dram("x1_s", [NK, 128, T], F32, kind=scratch_kind)
    out_s = dram("out_s", [NK, 128, T], F32, kind="ExternalOutput")
    if dbg:
        d_mrep = dram("d_mrep", [128, T], F32, kind="ExternalOutput")
        d_rrep = dram("d_rrep", [128, T], F32, kind="ExternalOutput")
        d_ln1 = dram("d_ln1", [128, NK * T], BF16, kind="ExternalOutput")

    TOP = ExitStack()
    with TOP:
        E = TOP.enter_context

        def sem(name):
            return E(nc.semaphore(name))

        onesb = E(nc.sbuf_tensor("onesb", [128, 128], BF16))
        onesf = E(nc.sbuf_tensor("onesf", [128, 128], F32))
        identb = E(nc.sbuf_tensor("identb", [128, 128], BF16))
        trif = E(nc.sbuf_tensor("trif", [128, 256], BF16))
        gwtile = E(nc.sbuf_tensor("gwtile", [128, NH], F32))
        cbt = E(nc.sbuf_tensor("cbt", [128, 8], F32))
        mrepA = E(nc.sbuf_tensor("mrepA", [128, T], F32))
        rrepA = E(nc.sbuf_tensor("rrepA", [128, T], F32))

        def bcol(i):
            return cbt[:, i:i + 1]

        s_bar = sem("s_bar")
        bar_n = [0]

        def barrier():
            with nc.Block() as blk:
                tgt = bar_n[0] + 5
                def mk(e):
                    e.drain()
                    e.sem_inc(s_bar, 1)
                    e.wait_ge(s_bar, tgt)
                blk.tensor(mk)
                blk.vector(mk)
                blk.scalar(mk)
                blk.gpsimd(mk)
                blk.sync(mk)
                bar_n[0] = tgt

        s_cst = sem("s_cst")
        s_cpl = sem("s_cpl")
        s_x0 = sem("s_x0")
        s_sq = sem("s_sq")
        s_mst = sem("s_mst")
        s_v0 = sem("s_v0")
        s_ln1 = sem("s_ln1")
        s_wh = [sem("s_wh0"), sem("s_wh1")]
        s_mmP = sem("s_mmP")
        s_evP = sem("s_evP")
        s_stP = [sem("s_stP0"), sem("s_stP1"), sem("s_stP2")]
        s_ev1 = sem("s_ev1")
        s_ev2 = sem("s_ev2")

        with ExitStack() as ph01:
            E01 = ph01.enter_context
            xb = E01(nc.sbuf_tensor("xb", [128, NK * T], BF16))
            mrep = E01(nc.sbuf_tensor("mrep", [128, T], F32))
            rrep = E01(nc.sbuf_tensor("rrep", [128, T], F32))
            wbuf = [E01(nc.sbuf_tensor(f"wbuf{i}", [128, NK * 1024], BF16)) for i in range(2)]

            with ExitStack() as ph:
                Ep = ph.enter_context
                xsq = [Ep(nc.sbuf_tensor(f"xsq{i}", [128, T], BF16)) for i in range(2)]
                t0a = Ep(nc.sbuf_tensor("t0a", [128, T], F32))
                t0b = Ep(nc.sbuf_tensor("t0b", [128, T], F32))
                ssum = [Ep(nc.psum_tensor(f"ssum{i}", [128, 512], F32)) for i in range(NSL)]
                ssq = [Ep(nc.psum_tensor(f"ssq{i}", [128, 512], F32)) for i in range(NSL)]
                blk = Ep(nc.Block())

                @blk.gpsimd
                def _(g):
                    g.dma_start(out=onesb[:], in_=cones).then_inc(s_cst, 16)
                    g.dma_start(out=identb[:], in_=cident).then_inc(s_cst, 16)
                    g.dma_start(out=trif[:], in_=ctri).then_inc(s_cst, 16)
                    for k in range(NK):
                        g.dma_start(out=xb[:, k * T:(k + 1) * T], in_=x_fm[k]).then_inc(s_x0, 16)
                    for k in range(NK):
                        g.dma_start(out=wbuf[0][:, k * 1024:(k + 1) * 1024],
                                    in_=wq[k][:, 0:1024]).then_inc(s_wh[0], 16)

                @blk.sync
                def _(sy):
                    sy.dma_start(out=onesf[:], in_=cones).then_inc(s_cpl, 16)
                    sy.dma_start(out=gwtile[:], in_=gwt).then_inc(s_cpl, 16)
                    sy.dma_start(out=cbt[:], in_=cbias).then_inc(s_cpl, 16)

                @blk.scalar
                def _(s_):
                    _preload_set(nc, s_, "natural_log_exp_and_others")
                    s_.wait_ge(s_x0, 16 * NK)
                    for k in range(NK):
                        if k >= 2:
                            s_.wait_ge(s_mst, k - 1)
                        s_.activation(xsq[k % 2][:], xb[:, k * T:(k + 1) * T], AF.Square).then_inc(s_sq, 1)
                    s_.wait_ge(s_cpl, 48)
                    for ns in range(NSL):
                        s_.wait_ge(s_v0, ns + 1)
                        sl = slice(ns * 512, (ns + 1) * 512)
                        s_.activation(t0a[:, sl], t0b[:, sl], AF.Ln, bias=bcol(0))
                        s_.drain()
                        s_.activation(rrep[:, sl], t0a[:, sl], AF.Exp, scale=-0.5)

                @blk.tensor
                def _(t):
                    t.wait_ge(s_cst, 48)
                    t.wait_ge(s_x0, 16 * NK)
                    for k in range(NK):
                        t.wait_ge(s_sq, k + 1)
                        for ns in range(NSL):
                            sl = slice(ns * 512, (ns + 1) * 512)
                            t.matmul(ssum[ns][:], onesb[:], xb[:, k * T:(k + 1) * T][:, sl],
                                     start=(k == 0), stop=(k == NK - 1))
                            mm = t.matmul(ssq[ns][:], onesb[:], xsq[k % 2][:, sl],
                                          start=(k == 0), stop=(k == NK - 1))
                        mm.then_inc(s_mst, 1)

                @blk.vector
                def _(v):
                    for ns in range(NSL):
                        v.wait_ge(s_mst, NK)
                        sl = slice(ns * 512, (ns + 1) * 512)
                        v.tensor_scalar(mrep[:, sl], ssum[ns][:], 1.0 / H, None, ALU.mult)
                        v.drain()
                        v.tensor_tensor(t0a[:, sl], mrep[:, sl], mrep[:, sl], ALU.mult)
                        v.drain()
                        v.scalar_tensor_tensor(t0b[:, sl], ssq[ns][:], 1.0 / H, t0a[:, sl],
                                               ALU.mult, ALU.subtract).then_inc(s_v0, 1)

            barrier()

            NT = T // 512
            with ExitStack() as ph:
                Ep = ph.enter_context
                t1a = Ep(nc.sbuf_tensor("t1a", [128, T], F32))
                stq = [Ep(nc.sbuf_tensor(f"stq{i}", [128, 512], BF16)) for i in range(3)]
                stf = [Ep(nc.sbuf_tensor(f"stf{i}", [128, 512], F32)) for i in range(3)]
                e_ta = [Ep(nc.sbuf_tensor(f"e_ta{i}", [128, 512], F32)) for i in range(3)]
                e_tb = [Ep(nc.sbuf_tensor(f"e_tb{i}", [128, 512], F32)) for i in range(3)]
                e_tc = [Ep(nc.sbuf_tensor(f"e_tc{i}", [128, 512], F32)) for i in range(3)]
                e_td = Ep(nc.sbuf_tensor("e_td", [128, 512], F32))
                pp = [Ep(nc.psum_tensor(f"pp{i}", [128, 512], F32)) for i in range(6)]
                blk = Ep(nc.Block())

                groups = []
                for mat in range(3):
                    for hh in range(2):
                        hidx = mat * 2 + hh
                        if mat < 2:
                            for d8 in range(8):
                                for nt in range(NT):
                                    groups.append((mat, hidx, hh * 8 + d8, nt))
                        else:
                            for tb in range(T // 128):
                                for nt in range(2):
                                    groups.append((mat, hidx, tb, nt))
                wsrc = [wq, wf, wi]
                _fprefix = []
                _nf = 0
                for _g in groups:
                    if _g[0] == 1:
                        _nf += 1
                    _fprefix.append(_nf)

                def _fcount(gi):
                    return _fprefix[gi]
                gend = {}
                acc = 0
                for mat in range(3):
                    for hh in range(2):
                        hidx = mat * 2 + hh
                        acc += 8 * NT if mat < 2 else (T // 128) * 2
                        gend[hidx] = acc

                @blk.vector
                def _(v):
                    for k in range(NK):
                        xk = xb[:, k * T:(k + 1) * T]
                        v.tensor_tensor(t1a[:], xk, mrep[:], ALU.subtract)
                        v.drain()
                        v.tensor_tensor(xk, t1a[:], rrep[:], ALU.mult).then_inc(s_ln1, 1)
                        v.drain()
                    nf = 0
                    for gi, (mat, hidx, dd, nt) in enumerate(groups):
                        if mat != 1:
                            continue
                        nf += 1
                        v.wait_ge(s_ev1, nf)
                        v.tensor_tensor(e_td[:], pp[gi % 6][:], e_ta[gi % 3][:], ALU.subtract)
                        v.drain()
                        v.scalar_tensor_tensor(stf[gi % 3][:], e_td[:], 0.5, e_tc[gi % 3][:],
                                               ALU.mult, ALU.subtract).then_inc(s_ev2, 1)
                        v.drain()

                @blk.gpsimd
                def _(g):
                    for mat in range(3):
                        for hh in range(2):
                            hidx = mat * 2 + hh
                            if hidx == 0:
                                continue
                            if hidx >= 2:
                                g.wait_ge(s_mmP, gend[hidx - 2])
                            for k in range(NK):
                                g.dma_start(out=wbuf[hidx % 2][:, k * 1024:(k + 1) * 1024],
                                            in_=wsrc[mat][k][:, hh * 1024:(hh + 1) * 1024]
                                            ).then_inc(s_wh[hidx % 2], 16)

                @blk.tensor
                def _(t):
                    for gi, (mat, hidx, dd, nt) in enumerate(groups):
                        if gi == 2:
                            t.wait_ge(s_ln1, NK)
                        if gi == 0 or groups[gi - 1][1] != hidx:
                            t.wait_ge(s_wh[hidx % 2], 16 * NK * (hidx // 2 + 1))
                        if gi >= 6:
                            t.wait_ge(s_evP, gi - 5)
                        wb = wbuf[hidx % 2]
                        if mat < 2:
                            dlo = (dd % 8) * 128
                            for k in range(NK):
                                if gi < 2:
                                    t.wait_ge(s_ln1, k + 1)
                                mm = t.matmul(pp[gi % 6][:],
                                              wb[:, k * 1024 + dlo:k * 1024 + dlo + 128],
                                              xb[:, k * T + nt * 512: k * T + nt * 512 + 512],
                                              start=(k == 0), stop=(k == NK - 1))
                        else:
                            for k in range(NK):
                                mm = t.matmul(pp[gi % 6][:],
                                              xb[:, k * T + dd * 128: k * T + dd * 128 + 128],
                                              wb[:, k * 1024 + nt * 512: k * 1024 + nt * 512 + 512],
                                              start=(k == 0), stop=(k == NK - 1))
                        mm.then_inc(s_mmP, 1)

                cum = {0: [0] * len(groups), 1: [0] * len(groups), 2: [0] * len(groups)}
                tot = [0, 0, 0]
                for gi, (mat, hidx, dd, nt) in enumerate(groups):
                    tot[gi % 3] += 32 if mat == 1 else 16
                    for par in range(3):
                        cum[par][gi] = tot[par]

                def _cm(tensor_s, dd, nt):
                    return tensor_s[4 * nt:4 * nt + 4, :, dd * 128:(dd + 1) * 128].rearrange(
                        "c p t -> p c t")

                @blk.scalar
                def _(s_):
                    s_.preload_act_table(AF.Silu)
                    for gi, (mat, hidx, dd, nt) in enumerate(groups):
                        if gi > 0 and groups[gi - 1][0] == 0 and mat == 1:
                            _preload_set(nc, s_, "natural_log_exp_and_others")
                        s_.wait_ge(s_mmP, gi + 1)
                        if gi >= 3:
                            s_.wait_ge(s_stP[gi % 3], cum[gi % 3][gi - 3])
                        if mat == 0:
                            s_.activation(stq[gi % 3][:], pp[gi % 6][:], AF.Silu).then_inc(s_evP, 1)
                        elif mat == 1:
                            s_.activation(e_ta[gi % 3][:], pp[gi % 6][:], AF.Abs)
                            s_.drain()
                            s_.activation(e_tb[gi % 3][:], e_ta[gi % 3][:], AF.Exp, scale=-1.0)
                            s_.drain()
                            s_.activation(e_tc[gi % 3][:], e_tb[gi % 3][:], AF.Ln,
                                          bias=bcol(1)).then_inc(s_ev1, 1)
                            s_.wait_ge(s_ev2, _fcount(gi))
                            s_.activation(e_tb[gi % 3][:], stf[gi % 3][:], AF.Exp)
                            s_.drain()
                            s_.activation(stq[gi % 3][:], e_tb[gi % 3][:], AF.Copy,
                                          scale=-1.0, bias=1.0).then_inc(s_evP, 1)
                        else:
                            s_.copy(stq[gi % 3][:], pp[gi % 6][:]).then_inc(s_evP, 1)

                @blk.sync
                def _(sy):
                    if dbg:
                        sy.wait_ge(s_ln1, NK)
                        sy.dma_start(out=d_mrep, in_=mrep[:]).then_inc(s_cpl, 16)
                        sy.dma_start(out=d_rrep, in_=rrep[:]).then_inc(s_cpl, 16)
                        sy.dma_start(out=d_ln1, in_=xb[:]).then_inc(s_cpl, 16)
                        sy.wait_ge(s_cpl, 64 + 48)
                    nstores = [0, 0, 0]
                    for gi, (mat, hidx, dd, nt) in enumerate(groups):
                        sy.wait_ge(s_evP, gi + 1)
                        if mat == 0:
                            sy.dma_start(out=_cm(q_s, dd, nt),
                                         in_=stq[gi % 3][:]).then_inc(s_stP[gi % 3], 16)
                            nstores[gi % 3] += 16
                        elif mat == 1:
                            sy.dma_start(out=_cm(g_s, dd, nt),
                                         in_=stf[gi % 3][:]).then_inc(s_stP[gi % 3], 16)
                            sy.dma_start(out=_cm(k_s, dd, nt),
                                         in_=stq[gi % 3][:]).then_inc(s_stP[gi % 3], 16)
                            nstores[gi % 3] += 32
                        else:
                            hh2 = hidx - 4
                            dst = v_s[dd][:, hh2 * 1024 + nt * 512: hh2 * 1024 + (nt + 1) * 512]
                            sy.dma_start(out=dst, in_=stq[gi % 3][:]).then_inc(s_stP[gi % 3], 16)
                            nstores[gi % 3] += 16
                    sy.wait_ge(s_stP[0], nstores[0])
                    sy.wait_ge(s_stP[1], nstores[1])
                    sy.wait_ge(s_stP[2], nstores[2])

        barrier()
        if stop_after == 1:
            return nc

        s_sl = [sem("s_sl0"), sem("s_sl1")]
        s_slg = sem("s_slg")
        s_slk = sem("s_slk")
        s_slq = sem("s_slq")
        s_ae = sem("s_ae")
        s_ke = sem("s_ke")
        s_kd = sem("s_kd")
        s_vA = sem("s_vA")
        s_s1 = sem("s_s1")
        s_m1 = sem("s_m1")
        s_s2 = sem("s_s2")
        s_o1 = sem("s_o1")
        s_s3 = sem("s_s3")
        s_m3 = sem("s_m3")
        s_s4 = sem("s_s4")
        s_h4 = sem("s_h4")
        s_ost = sem("s_ost")
        s_pz = sem("s_pz")

        with ExitStack() as ph:
            Ep = ph.enter_context
            qsl = Ep(nc.sbuf_tensor("qsl", [128, 4 * H], BF16))
            vsl = [Ep(nc.sbuf_tensor(f"vsl{i}", [128, 4 * H], BF16)) for i in range(2)]
            gsl = Ep(nc.sbuf_tensor("gsl", [128, 4 * H], F32))
            ksl = Ep(nc.sbuf_tensor("ksl", [128, 4 * H], BF16))
            osl = Ep(nc.sbuf_tensor("osl", [128, NH * 512], BF16))
            hst = Ep(nc.sbuf_tensor("hst", [128, NH * 128], F32))
            hbf = Ep(nc.sbuf_tensor("hbf", [128, NH * 128], BF16))
            AA = [Ep(nc.sbuf_tensor(f"AA{i}", [128, NH * 128], F32)) for i in range(2)]
            e1t = Ep(nc.sbuf_tensor("e1t", [128, NH * 128], F32))
            e2t = Ep(nc.sbuf_tensor("e2t", [128, NH * 128], F32))
            qe = Ep(nc.sbuf_tensor("qe", [128, NH * 128], BF16))
            ke = Ep(nc.sbuf_tensor("ke", [128, NH * 128], BF16))
            kd = Ep(nc.sbuf_tensor("kd", [128, NH * 128], BF16))
            ssb = Ep(nc.sbuf_tensor("ssb", [128, NH * 128], BF16))
            ktm = Ep(nc.sbuf_tensor("ktm", [128, NH * 128], BF16))
            ec50 = Ep(nc.sbuf_tensor("ec50", [128, NH], F32))
            ecC = Ep(nc.sbuf_tensor("ecC", [128, NH], F32))
            sp = [Ep(nc.psum_tensor(f"sp{i}", [128, 256], F32)) for i in range(2)]
            op = [Ep(nc.psum_tensor(f"op{i}", [128, 256], F32)) for i in range(2)]
            tp = [Ep(nc.psum_tensor(f"tp{i}", [128, 512], BF16)) for i in range(2)]
            hp = [Ep(nc.psum_tensor(f"hp{i}", [128, 256], F32)) for i in range(2)]
            blk = Ep(nc.Block())

            def hsl(tile, hd, w=128):
                return tile[:, hd * w:(hd + 1) * w]

            def chv(tile, cc, hd):
                return tile[:, cc * H + hd * 128: cc * H + (hd + 1) * 128]

            @blk.sync
            def _(sy):
                def load_qv(s):
                    sy.dma_start(out=vsl[s % 2][:].rearrange("p (c t) -> p c t", c=4),
                                 in_=v_s[s * 4:(s + 1) * 4].rearrange("c p t -> p c t")
                                 ).then_inc(s_sl[s % 2], 16)

                def load_q(s):
                    sy.dma_start(out=qsl[:].rearrange("p (c t) -> p c t", c=4),
                                 in_=q_s[s * 4:(s + 1) * 4].rearrange("c p t -> p c t")
                                 ).then_inc(s_slq, 16)

                def load_g(s):
                    sy.dma_start(out=gsl[:].rearrange("p (c t) -> p c t", c=4),
                                 in_=g_s[s * 4:(s + 1) * 4].rearrange("c p t -> p c t")
                                 ).then_inc(s_slg, 16)

                def load_k(s):
                    sy.dma_start(out=ksl[:].rearrange("p (c t) -> p c t", c=4),
                                 in_=k_s[s * 4:(s + 1) * 4].rearrange("c p t -> p c t")
                                 ).then_inc(s_slk, 16)

                for s in range(min(2, NSL)):
                    load_qv(s)
                load_g(0)
                load_k(0)
                load_q(0)
                for s in range(NSL):
                    sy.wait_ge(s_o1, 32 * (s + 1))
                    t0 = s * 512
                    sy.dma_start(out=o_s[:, :, t0:t0 + 512].rearrange("h p t -> p h t"),
                                 in_=osl[:].rearrange("p (h t) -> p h t", h=NH)
                                 ).then_inc(s_ost, 16)
                    if s + 1 < NSL:
                        sy.wait_ge(s_vA, 4 * (s + 1))
                        load_g(s + 1)
                        sy.wait_ge(s_ke, 4 * (s + 1))
                        load_k(s + 1)
                        load_q(s + 1)
                    if s + 2 < NSL:
                        sy.wait_ge(s_h4, 64 * (s + 1))
                        load_qv(s + 2)
                sy.wait_ge(s_ost, 16 * NSL)

            @blk.scalar
            def _(s_):
                _preload_set(nc, s_, "natural_log_exp_and_others")
                for ci in range(CH):
                    s = ci // 4
                    cc = ci % 4
                    if ci >= 1:
                        s_.wait_ge(s_ke, ci)
                    s_.wait_ge(s_vA, ci + 1)
                    s_.activation(e2t[:], AA[ci % 2][:], AF.Exp, scale=-1.0, bias=bcol(3))
                    s_.drain()
                    s_.activation(e1t[:], AA[ci % 2][:], AF.Exp, scale=1.0, bias=bcol(2))
                    s_.drain()
                    aac = AA[ci % 2][:].rearrange("p (h t) -> p h t", h=NH)[:, :, 127:128]
                    s_.activation(ec50[:].rearrange("p (h o) -> p h o", o=1), aac,
                                  AF.Exp, bias=bcol(4)).then_inc(s_ae, 1)
                    if ci >= 1:
                        s_.wait_ge(s_s3, 4 * ci)
                    s_.wait_ge(s_ke, ci + 1)
                    for hd in range(NH):
                        s_.activation(hsl(kd, hd), hsl(ke, hd), AF.Copy,
                                      scale=ec50[:, hd:hd + 1])
                    s_.wait_ge(s_h4, 16 * ci)
                    s_.activation(hbf[:], hst[:], AF.Copy,
                                  scale=math.exp(-SH)).then_inc(s_kd, 1)
                    if s >= 1 and cc == 0:
                        s_.wait_ge(s_ost, 16 * s)
                    for pr in range(NH // 2):
                        s_.wait_ge(s_s2, 8 * ci + pr + 1)
                        dst = osl[:].rearrange("p (h t) -> p h t", h=NH)[
                            :, 2 * pr:2 * pr + 2, cc * 128:(cc + 1) * 128]
                        s_.copy(dst, op[pr % 2][:].rearrange("p (h t) -> p h t", h=2)
                                ).then_inc(s_o1, 1)

            @blk.vector
            def _(v):
                v.memset(hst[:], 0.0)
                v.memset(sp[0][:], 0.0)
                v.memset(sp[1][:], 0.0)
                v.drain()
                v.engine_nop().then_inc(s_pz, 1)
                for ci in range(CH):
                    s = ci // 4
                    cc = ci % 4
                    v.wait_ge(s_slg, 16 * (s + 1))
                    if ci >= 2:
                        v.wait_ge(s_ae, ci - 1)
                    for hd in range(NH):
                        v.tensor_tensor_scan(hsl(AA[ci % 2], hd), onesf[:], chv(gsl, cc, hd),
                                             0.0, ALU.mult, ALU.add)
                    v.drain()
                    v.tensor_single_scalar(AA[ci % 2][:], AA[ci % 2][:], ACL,
                                           ALU.max).then_inc(s_vA, 1)
                    v.wait_ge(s_slk, 16 * (s + 1))
                    v.wait_ge(s_slq, 16 * (s + 1))
                    if ci >= 1:
                        v.wait_ge(s_s2, 8 * ci)
                        v.wait_ge(s_kd, ci)
                    v.wait_ge(s_ae, ci + 1)
                    v.tensor_scalar(ecC[:], ec50[:], math.exp(-SH), None, ALU.mult)
                    v.tensor_tensor(ke[:], ksl[:, cc * H:(cc + 1) * H], e2t[:], ALU.mult)
                    v.tensor_tensor(qe[:], qsl[:, cc * H:(cc + 1) * H],
                                    e1t[:], ALU.mult).then_inc(s_ke, 1)
                    for pr in range(NH // 2):
                        v.wait_ge(s_s1, 8 * ci + pr + 1)
                        v.tensor_tensor(ssb[:, pr * 256:(pr + 1) * 256], sp[pr % 2][:], trif[:],
                                        ALU.mult).then_inc(s_m1, 1)
                    for qd in range(NH // 4):
                        v.wait_ge(s_s3, 4 * ci + qd + 1)
                        v.tensor_copy(ktm[:, qd * 512:(qd + 1) * 512],
                                      tp[qd % 2][:]).then_inc(s_m3, 1)
                    for hd in range(NH):
                        v.wait_ge(s_s4, 8 * ci + hd // 2 + 1)
                        v.scalar_tensor_tensor(hsl(hst, hd), hsl(hst, hd), ecC[:, hd:hd + 1],
                                               hp[(hd // 2) % 2][:, (hd % 2) * 128:(hd % 2 + 1) * 128],
                                               ALU.mult, ALU.add).then_inc(s_h4, 1)

            @blk.tensor
            def _(t):
                for ci in range(CH):
                    s = ci // 4
                    cc = ci % 4
                    t.wait_ge(s_ke, ci + 1)
                    t.wait_ge(s_sl[s % 2], 16 * (s // 2 + 1))
                    if ci == 0:
                        t.wait_ge(s_pz, 1)
                    for pr in range(NH // 2):
                        if 8 * ci + pr >= 2:
                            t.wait_ge(s_m1, 8 * ci + pr - 1)
                        for q2 in range(2):
                            hd = 2 * pr + q2
                            reg = sp[pr % 2][:, q2 * 128:(q2 + 1) * 128]
                            t.matmul(reg[0:64, :], hsl(ke, hd)[:, 0:64], hsl(qe, hd),
                                     start=True, stop=True)
                            mmq = t.matmul(reg[64:128, 64:128], hsl(ke, hd)[:, 64:128],
                                           hsl(qe, hd)[:, 64:128], start=True, stop=True)
                        mmq.then_inc(s_s1, 1)
                    t.wait_ge(s_kd, ci + 1)
                    for pr in range(NH // 2):
                        t.wait_ge(s_m1, 8 * ci + pr + 1)
                        if 8 * ci + pr >= 2:
                            t.wait_ge(s_o1, 8 * ci + pr - 1)
                        for q2 in range(2):
                            hd = 2 * pr + q2
                            reg = op[pr % 2][:, q2 * 128:(q2 + 1) * 128]
                            vv = chv(vsl[s % 2], cc, hd)
                            t.matmul(reg, vv, hsl(ssb, hd), start=True, stop=False)
                            mmq = t.matmul(reg, hsl(hbf, hd), hsl(qe, hd),
                                           start=False, stop=True)
                        mmq.then_inc(s_s2, 1)
                    for qd in range(NH // 4):
                        if 4 * ci + qd >= 2:
                            t.wait_ge(s_m3, 4 * ci + qd - 1)
                        for q4 in range(4):
                            hd = 4 * qd + q4
                            mmq = t.transpose(tp[qd % 2][:, q4 * 128:(q4 + 1) * 128],
                                              hsl(kd, hd), identb[:])
                        mmq.then_inc(s_s3, 1)
                    for pr in range(NH // 2):
                        t.wait_ge(s_m3, 4 * ci + pr // 2 + 1)
                        if 8 * ci + pr >= 2:
                            t.wait_ge(s_h4, 2 * (8 * ci + pr) - 2)
                        for q2 in range(2):
                            hd = 2 * pr + q2
                            vv = chv(vsl[s % 2], cc, hd)
                            mmq = t.matmul(hp[pr % 2][:, q2 * 128:(q2 + 1) * 128],
                                           hsl(ktm, hd), vv, start=True, stop=True)
                        mmq.then_inc(s_s4, 1)

        barrier()
        if stop_after == 2:
            return nc

        s_wo = sem("s_wo")
        s_sl3 = [sem("s_sl30"), sem("s_sl31")]
        s_slo3 = sem("s_slo3")
        s_sq3 = sem("s_sq3")
        s_st3 = sem("s_st3")
        s_rs3 = sem("s_rs3")
        s_oh3 = sem("s_oh3")
        s_mm3 = sem("s_mm3")
        s_ev3 = sem("s_ev3")
        s_x1st = [sem("s_x1st0"), sem("s_x1st1")]
        s_sq5 = sem("s_sq5")
        s_st4 = sem("s_st4")
        s_v5 = sem("s_v5")

        with ExitStack() as ph:
            Ep = ph.enter_context
            wob = Ep(nc.sbuf_tensor("wob", [128, NK * H], BF16))
            osl2s = Ep(nc.sbuf_tensor("osl2s", [128, NH * 512], BF16))
            xsl = [Ep(nc.sbuf_tensor(f"xsl{i}", [128, NK * 512], F32)) for i in range(2)]
            osq = Ep(nc.sbuf_tensor("osq", [128, NH * 512], BF16))
            ohat = Ep(nc.sbuf_tensor("ohat", [128, NH * 512], BF16))
            rstd3 = Ep(nc.sbuf_tensor("rstd3", [128, 512], F32))
            stg3 = [Ep(nc.sbuf_tensor(f"stg3{i}", [128, 512], F32)) for i in range(2)]
            xb3 = [Ep(nc.sbuf_tensor(f"xb3{i}", [128, 512], BF16)) for i in range(2)]
            xq3 = [Ep(nc.sbuf_tensor(f"xq3{i}", [128, 512], BF16)) for i in range(2)]
            t3b = Ep(nc.sbuf_tensor("t3b", [128, 512], F32))
            st3 = [Ep(nc.psum_tensor(f"st3{i}", [128, 512], F32)) for i in range(2)]
            pw = [Ep(nc.psum_tensor(f"pw{i}", [128, 512], F32)) for i in range(2)]
            st4s = Ep(nc.psum_tensor("st4s", [128, 512], F32))
            st4q = Ep(nc.psum_tensor("st4q", [128, 512], F32))
            blk = Ep(nc.Block())

            @blk.gpsimd
            def _(g):
                for k in range(NK):
                    g.dma_start(out=wob[:, k * H:(k + 1) * H], in_=wo[k]).then_inc(s_wo, 16)

            @blk.sync
            def _(sy):
                def load_o(s):
                    t0 = s * 512
                    sy.dma_start(out=osl2s[:].rearrange("p (h t) -> p h t", h=NH),
                                 in_=o_s[:, :, t0:t0 + 512].rearrange("h p t -> p h t")).then_inc(s_slo3, 16)
                def load_x(s):
                    t0 = s * 512
                    sy.dma_start(out=xsl[s % 2][:].rearrange("p (k t) -> p k t", k=NK),
                                 in_=x_fm[:, :, t0:t0 + 512].rearrange("k p t -> p k t")).then_inc(s_sl3[s % 2], 16)
                load_o(0)
                for s in range(min(2, NSL)):
                    load_x(s)
                nst = [0, 0]
                for s in range(NSL):
                    t0 = s * 512
                    if s + 1 < NSL:
                        sy.wait_ge(s_oh3, s + 1)
                        load_o(s + 1)
                    for d in range(NK):
                        gd = NK * s + d
                        sy.wait_ge(s_ev3, gd + 1)
                        sy.dma_start(out=x1_s[d][:, t0:t0 + 512],
                                     in_=stg3[gd % 2][:]).then_inc(s_x1st[gd % 2], 16)
                        nst[gd % 2] += 16
                    if s + 2 < NSL:
                        load_x(s + 2)
                sy.wait_ge(s_x1st[0], nst[0])
                sy.wait_ge(s_x1st[1], nst[1])

            @blk.scalar
            def _(s_):
                _preload_set(nc, s_, "natural_log_exp_and_others")
                for s in range(NSL):
                    s_.wait_ge(s_slo3, 16 * (s + 1))
                    s_.activation(osq[:], osl2s[:], AF.Square).then_inc(s_sq3, 1)
                    s_.wait_ge(s_st3, s + 1)
                    s_.activation(rstd3[:], st3[s % 2][:], AF.Ln, scale=1.0 / H, bias=bcol(0))
                    s_.drain()
                    s_.activation(rstd3[:], rstd3[:], AF.Exp, scale=-0.5).then_inc(s_rs3, 1)
                    for d in range(NK):
                        gd = NK * s + d
                        s_.wait_ge(s_ev3, gd + 1)
                        if gd >= 2:
                            s_.wait_ge(s_st4, gd - 1)
                        s_.copy(xb3[gd % 2][:], stg3[gd % 2][:])
                        s_.drain()
                        s_.activation(xq3[gd % 2][:], stg3[gd % 2][:], AF.Square).then_inc(s_sq5, 1)
                    s_.wait_ge(s_v5, s + 1)
                    sl5 = slice(s * 512, (s + 1) * 512)
                    s_.activation(t3b[:], t3b[:], AF.Ln, bias=bcol(0))
                    s_.drain()
                    s_.activation(rrepA[:, sl5], t3b[:], AF.Exp, scale=-0.5)

            @blk.vector
            def _(v):
                for s in range(NSL):
                    v.wait_ge(s_rs3, s + 1)
                    v.wait_ge(s_sl3[s % 2], 16 * (s // 2 + 1))
                    for hd in range(NH):
                        a = v.scalar_tensor_tensor(ohat[:, hd * 512:(hd + 1) * 512],
                                                   osl2s[:, hd * 512:(hd + 1) * 512],
                                                   gwtile[:, hd:hd + 1], rstd3[:],
                                                   ALU.mult, ALU.mult)
                    a.then_inc(s_oh3, 1)
                    for d in range(NK):
                        gd = NK * s + d
                        v.wait_ge(s_mm3, gd + 1)
                        if gd >= 2:
                            v.wait_ge(s_x1st[gd % 2], 16 * (gd // 2))
                            v.wait_ge(s_sq5, gd - 1)
                        v.tensor_tensor(stg3[gd % 2][:], pw[d % 2][:],
                                        xsl[s % 2][:, d * 512:(d + 1) * 512], ALU.add).then_inc(s_ev3, 1)
                    v.wait_ge(s_st4, NK * (s + 1))
                    sl5 = slice(s * 512, (s + 1) * 512)
                    v.tensor_scalar(mrepA[:, sl5], st4s[:], 1.0 / H, None, ALU.mult)
                    v.drain()
                    v.tensor_tensor(t3b[:], mrepA[:, sl5], mrepA[:, sl5], ALU.mult)
                    v.drain()
                    v.scalar_tensor_tensor(t3b[:], st4q[:], 1.0 / H, t3b[:],
                                           ALU.mult, ALU.subtract).then_inc(s_v5, 1)
                    v.drain()

            @blk.tensor
            def _(t):
                t.wait_ge(s_wo, 16 * NK)
                for s in range(NSL):
                    t.wait_ge(s_sq3, s + 1)
                    if s >= 2:
                        t.wait_ge(s_rs3, s - 1)
                        t.wait_ge(s_ev3, NK * (s - 1))
                    for hd in range(NH):
                        mm = t.matmul(st3[s % 2][:], onesb[:], osq[:, hd * 512:(hd + 1) * 512],
                                      start=(hd == 0), stop=(hd == NH - 1))
                    mm.then_inc(s_st3, 1)
                    t.wait_ge(s_oh3, s + 1)
                    for d in range(NK):
                        gd = NK * s + d
                        if gd >= 2:
                            t.wait_ge(s_ev3, gd - 1)
                        for k in range(NK):
                            mm = t.matmul(pw[d % 2][:], wob[:, k * H + d * 128: k * H + d * 128 + 128],
                                          ohat[:, k * 512:(k + 1) * 512],
                                          start=(k == 0), stop=(k == NK - 1))
                        mm.then_inc(s_mm3, 1)
                        if d >= 2:
                            gd2 = gd - 2
                            t.wait_ge(s_sq5, gd2 + 1)
                            t.matmul(st4s[:], onesb[:], xb3[gd2 % 2][:],
                                     start=(d == 2), stop=False)
                            mm = t.matmul(st4q[:], onesb[:], xq3[gd2 % 2][:],
                                          start=(d == 2), stop=False)
                            mm.then_inc(s_st4, 1)
                    for dd2 in range(NK - 2, NK):
                        gd2 = NK * s + dd2
                        t.wait_ge(s_sq5, gd2 + 1)
                        t.matmul(st4s[:], onesb[:], xb3[gd2 % 2][:],
                                 start=False, stop=(dd2 == NK - 1))
                        mm = t.matmul(st4q[:], onesb[:], xq3[gd2 % 2][:],
                                      start=False, stop=(dd2 == NK - 1))
                        mm.then_inc(s_st4, 1)

        barrier()
        if stop_after == 3:
            return nc

        s_yi = sem("s_yi")
        s_r4 = sem("s_r4")
        s_l4 = sem("s_l4")
        s_gl = sem("s_gl")
        s_dg = sem("s_dg")
        s_pd = sem("s_pd")
        s_ln2 = sem("s_ln2")
        s_w4 = [sem("s_w40"), sem("s_w41")]
        s_wd4 = sem("s_wd4")
        s_mmA = sem("s_mmA")
        s_evS = sem("s_evS")
        s_evA = sem("s_evA")
        s_mmB = sem("s_mmB")
        s_evB = sem("s_evB")
        s_out = sem("s_out")

        cnt = dict(yi=0, r4=0, l4=0, gl=0, dg=0, pd=0, ln2=0,
                   w4=[0, 0], wd4=0, mmA=0, evS=0, evA=0, mmB=0, evB=0, out=0)
        NB4 = NE * NF // RBAND

        with ExitStack() as ph4:
            E4 = ph4.enter_context
            Y_acc = E4(nc.sbuf_tensor("Y_acc", [128, NK * PT], F32))
            ln2 = E4(nc.sbuf_tensor("ln2", [128, NK * PT], BF16))
            w0rep = E4(nc.sbuf_tensor("w0rep", [128, PT], BF16))
            w1rep = E4(nc.sbuf_tensor("w1rep", [128, PT], BF16))
            d4 = E4(nc.sbuf_tensor("d4", [128, PT], F32))
            hb4 = [E4(nc.sbuf_tensor(f"hb4_{i}", [128, RBAND * PT], BF16)) for i in range(2)]
            wgb = [E4(nc.sbuf_tensor(f"wgb{i}", [128, H], BF16)) for i in range(2)]
            wub = [E4(nc.sbuf_tensor(f"wub{i}", [128, H], BF16)) for i in range(2)]
            wdb = E4(nc.sbuf_tensor("wdb", [128, RBAND * H], BF16))
            tmp4 = E4(nc.sbuf_tensor("tmp4", [128, 2 * 512], BF16))
            t4a = E4(nc.sbuf_tensor("t4a", [128, PT], F32))
            g0b = hb4[0][:, 0:NK * 128]
            g1b = hb4[0][:, NK * 128:2 * NK * 128]
            dgb = hb4[1][:, 0:NK * 128]
            gps = [E4(nc.psum_tensor(f"gps{i}", [128, 512], F32)) for i in range(2)]
            ups = [E4(nc.psum_tensor(f"ups{i}", [128, 512], F32)) for i in range(2)]
            yps = [E4(nc.psum_tensor(f"yps{i}", [128, 512], F32)) for i in range(2)]

            chunks = [(e, f) for e in range(NE) for f in range(NF)]
            wrep = [w0rep, w1rep]

            for p in range(NP):
                t0p = p * PT
                with ExitStack() as ph:
                    Ep = ph.enter_context
                    blk = Ep(nc.Block())
                    base = dict((k2, (list(v2) if isinstance(v2, list) else v2))
                                for k2, v2 in cnt.items())

                    @blk.sync
                    def _(sy):
                        if p == 0:
                            for k in range(NK):
                                sy.dma_start(out=Y_acc[:, k * PT:(k + 1) * PT],
                                             in_=x1_s[k][:, t0p:t0p + PT]).then_inc(s_yi, 16)
                        for d in range(NK):
                            sy.wait_ge(s_evB, base["evB"] + 32 * (NB4 - 1) + 2 * d + 2)
                            sy.dma_start(out=out_s[d][:, t0p:t0p + PT],
                                         in_=Y_acc[:, d * PT:(d + 1) * PT]).then_inc(s_out, 16)
                            if p + 1 < NP:
                                sy.wait_ge(s_out, base["out"] + 16 * (d + 1))
                                sy.dma_start(out=Y_acc[:, d * PT:(d + 1) * PT],
                                             in_=x1_s[d][:, t0p + PT:t0p + 2 * PT]).then_inc(s_yi, 16)
                        sy.wait_ge(s_out, base["out"] + 16 * NK)

                    @blk.gpsimd
                    def _(g):
                        if p >= 1:
                            g.wait_ge(s_mmB, base["mmB"])
                        g.dma_start(out=g0b.rearrange("p (k j) -> p k j", k=NK),
                                    in_=grep[0].rearrange("k p j -> p k j")).then_inc(s_gl, 16)
                        g.dma_start(out=g1b.rearrange("p (k j) -> p k j", k=NK),
                                    in_=grep[1].rearrange("k p j -> p k j")).then_inc(s_gl, 16)
                        for b in range(NB4):
                            for ii in range(RBAND):
                                fc = b * RBAND + ii
                                e, f = chunks[fc]
                                if fc >= 2:
                                    g.wait_ge(s_mmA, base["mmA"] + 2 * fc - 2)
                                g.dma_start(out=wgb[fc % 2][:], in_=wgt[e, f]).then_inc(s_w4[fc % 2], 16)
                                g.dma_start(out=wub[fc % 2][:], in_=wut[e, f]).then_inc(s_w4[fc % 2], 16)
                            if b >= 1:
                                g.wait_ge(s_mmB, base["mmB"] + 32 * b)
                            for ii in range(RBAND):
                                e, f = chunks[b * RBAND + ii]
                                g.dma_start(out=wdb[:, ii * H:(ii + 1) * H],
                                            in_=wdt[e, f]).then_inc(s_wd4, 16)

                    @blk.scalar
                    def _(s_):
                        _preload_set(nc, s_, "natural_log_exp_and_others")
                        for ns in range(NS4):
                            s_.wait_ge(s_l4, base["l4"] + ns + 1)
                            sl = slice(ns * 512, (ns + 1) * 512)
                            s_.activation(t4a[:, sl], d4[:, sl], AF.Abs)
                            s_.drain()
                            s_.activation(t4a[:, sl], t4a[:, sl], AF.Exp, scale=-1.0).then_inc(s_r4, 1)
                        _preload_set(nc, s_, "silu_and_others")
                        for j in range(2 * NE * NF):
                            s_.wait_ge(s_mmA, base["mmA"] + j + 1)
                            if j >= 2:
                                s_.wait_ge(s_evA, base["evA"] + j - 1)
                            s_.activation(tmp4[:, (j % 2) * 512:(j % 2 + 1) * 512], gps[j % 2][:],
                                          AF.Silu).then_inc(s_evS, 1)

                    @blk.tensor
                    def _(t):
                        def b_group(bb, gg2):
                            gl = 32 * bb + gg2
                            d, sx = gg2 // 2, gg2 % 2
                            if gl >= 2:
                                t.wait_ge(s_evB, base["evB"] + gl - 1)
                            for ii in range(RBAND):
                                mm = t.matmul(yps[gl % 2][:], wdb[:, ii * H + d * 128: ii * H + (d + 1) * 128],
                                              hb4[bb % 2][:, ii * PT + sx * 512: ii * PT + (sx + 1) * 512],
                                              start=(ii == 0), stop=(ii == RBAND - 1))
                            mm.then_inc(s_mmB, 1)
                        for b in range(NB4):
                            for jj in range(2 * RBAND):
                                j = 2 * RBAND * b + jj
                                fc, sx = j // 2, j % 2
                                if sx == 0:
                                    t.wait_ge(s_w4[fc % 2], base["w4"][fc % 2] + 32 * (fc // 2 + 1))
                                if j >= 2:
                                    t.wait_ge(s_evA, base["evA"] + j - 1)
                                    t.wait_ge(s_evS, base["evS"] + j - 1)
                                for k in range(NK):
                                    if j < 2:
                                        t.wait_ge(s_ln2, base["ln2"] + k + 1)
                                    t.matmul(gps[j % 2][:], wgb[fc % 2][:, k * 128:(k + 1) * 128],
                                             ln2[:, k * PT + sx * 512: k * PT + (sx + 1) * 512],
                                             start=(k == 0), stop=(k == NK - 1))
                                for k in range(NK):
                                    mm = t.matmul(ups[j % 2][:], wub[fc % 2][:, k * 128:(k + 1) * 128],
                                                  ln2[:, k * PT + sx * 512: k * PT + (sx + 1) * 512],
                                                  start=(k == 0), stop=(k == NK - 1))
                                mm.then_inc(s_mmA, 1)
                                if b == 0 and jj == 1:
                                    t.wait_ge(s_dg, base["dg"] + 1)
                                    t.wait_ge(s_ln2, base["ln2"] + NK)
                                    if p >= 1:
                                        t.wait_ge(s_evB, base["evB"])
                                    for ns in range(NS4):
                                        sl = slice(ns * 512, (ns + 1) * 512)
                                        for k in range(NK):
                                            mm = t.matmul(yps[ns][:], dgb[:, k * 128:(k + 1) * 128],
                                                          ln2[:, k * PT:(k + 1) * PT][:, sl],
                                                          start=(k == 0), stop=(k == NK - 1))
                                        mm.then_inc(s_pd, 1)
                            t.wait_ge(s_evA, base["evA"] + 2 * RBAND * (b + 1))
                            t.wait_ge(s_wd4, base["wd4"] + 16 * RBAND * (b + 1))
                            for gg2 in range(32):
                                b_group(b, gg2)

                    @blk.vector
                    def _(v):
                        v.wait_ge(s_yi, base["yi"] + 16 * NK)
                        for k in range(NK):
                            yk = Y_acc[:, k * PT:(k + 1) * PT]
                            v.tensor_tensor(t4a[:], yk, mrepA[:, t0p:t0p + PT], ALU.subtract)
                            v.drain()
                            v.tensor_tensor(ln2[:, k * PT:(k + 1) * PT], t4a[:], rrepA[:, t0p:t0p + PT],
                                            ALU.mult).then_inc(s_ln2, 1)
                            v.drain()
                        v.wait_ge(s_gl, 32 * (p + 1))
                        if p >= 1:
                            v.wait_ge(s_mmB, base["mmB"])
                        v.tensor_tensor(dgb, g0b, g1b, ALU.subtract)
                        v.drain()
                        v.engine_nop().then_inc(s_dg, 1)
                        for ns in range(NS4):
                            v.wait_ge(s_pd, base["pd"] + ns + 1)
                            sl = slice(ns * 512, (ns + 1) * 512)
                            v.tensor_scalar(d4[:, sl], yps[ns][:], 1.0, None,
                                            ALU.mult).then_inc(s_l4, 1)
                            v.wait_ge(s_r4, base["r4"] + ns + 1)
                            v.tensor_scalar(t4a[:, sl], t4a[:, sl], 1.0, None, ALU.add)
                            v.drain()
                            v.reciprocal(t4a[:, sl], t4a[:, sl])
                            v.drain()
                            v.tensor_single_scalar(d4[:, sl], d4[:, sl], 0.0, ALU.is_ge)
                            v.drain()
                            v.tensor_tensor(w0rep[:, sl], t4a[:, sl], d4[:, sl], ALU.mult)
                            v.drain()
                            v.tensor_tensor(w1rep[:, sl], t4a[:, sl], w0rep[:, sl], ALU.subtract)
                            v.drain()
                        def y_adds(bb):
                            for gg2 in range(32):
                                gl = 32 * bb + gg2
                                d, sx = gg2 // 2, gg2 % 2
                                v.wait_ge(s_mmB, base["mmB"] + gl + 1)
                                ya = Y_acc[:, d * PT + sx * 512: d * PT + (sx + 1) * 512]
                                v.tensor_tensor(ya, ya, yps[gl % 2][:], ALU.add).then_inc(s_evB, 1)
                        for b in range(NB4):
                            for jj in range(2 * RBAND):
                                j = 2 * RBAND * b + jj
                                fc, sx = j // 2, j % 2
                                e, f = chunks[fc]
                                v.wait_ge(s_evS, base["evS"] + j + 1)
                                tsl = tmp4[:, (j % 2) * 512:(j % 2 + 1) * 512]
                                v.tensor_tensor(tsl, tsl, ups[j % 2][:], ALU.mult)
                                v.drain()
                                hv = hb4[b % 2][:, (fc % RBAND) * PT + sx * 512:(fc % RBAND) * PT + (sx + 1) * 512]
                                v.tensor_tensor(hv, tsl, wrep[e][:, sx * 512:(sx + 1) * 512],
                                                ALU.mult).then_inc(s_evA, 1)
                                v.drain()
                            y_adds(b)

                    cnt["yi"] += 16 * NK
                    cnt["gl"] += 32
                    cnt["r4"] += NS4
                    cnt["l4"] += NS4
                    cnt["dg"] += 1
                    cnt["pd"] += NS4
                    cnt["ln2"] += NK
                    cnt["mmA"] += 2 * NE * NF
                    cnt["evS"] += 2 * NE * NF
                    cnt["evA"] += 2 * NE * NF
                    cnt["mmB"] += 32 * NB4
                    cnt["evB"] += 32 * NB4
                    cnt["w4"] = [cnt["w4"][i] + 32 * (NE * NF // 2) for i in range(2)]
                    cnt["wd4"] += 16 * NE * NF
                    cnt["out"] += 16 * NK

    return nc


def _layout_inputs(L0_attn_x, L0_attn_Wq, L0_attn_Wf, L0_attn_Wi, L0_attn_Wo,
                   L0_attn_g_weight, L0_ffn_gate_w, L0_ffn_w_gate, L0_ffn_w_up,
                   L0_ffn_w_down, T=2048):
    def kslab(W):
        return np.ascontiguousarray(np.asarray(W, np.float32).reshape(NK, 128, H))

    w_gate = np.asarray(L0_ffn_w_gate, np.float32)
    w_up = np.asarray(L0_ffn_w_up, np.float32)
    w_down = np.asarray(L0_ffn_w_down, np.float32)
    gate_w = np.asarray(L0_ffn_gate_w, np.float32)
    shared = {
        "wq": kslab(L0_attn_Wq),
        "wf": kslab(L0_attn_Wf),
        "wi": kslab(L0_attn_Wi),
        "wo": kslab(L0_attn_Wo),
        "wgt": np.ascontiguousarray(
            w_gate.reshape(NE, NF, 128, NK, 128).transpose(0, 1, 4, 3, 2)
        ).reshape(NE, NF, 128, H),
        "wut": np.ascontiguousarray(
            w_up.reshape(NE, NF, 128, NK, 128).transpose(0, 1, 4, 3, 2)
        ).reshape(NE, NF, 128, H),
        "wdt": np.ascontiguousarray(
            w_down.reshape(NE, NK, 128, NF, 128).transpose(0, 3, 4, 1, 2)
        ).reshape(NE, NF, 128, H),
        "grep": np.ascontiguousarray(
            np.broadcast_to(gate_w.reshape(NE, NK, 128, 1), (NE, NK, 128, 128))
        ).astype(np.float32),
        "gwt": np.ascontiguousarray(np.asarray(L0_attn_g_weight, np.float32).reshape(NH, 128).T),
        "cones": np.ones((128, 128), np.float32),
        "cident": np.eye(128, dtype=np.float32),
        "ctri": np.tile(np.triu(np.ones((128, 128), np.float32)), (1, 2)),
        "cbias": np.tile(np.array([EPS, 1.0, SH + LNC, -SH, SH, 0.0, 0.0, 0.0],
                                  np.float32), (128, 1)),
    }
    ncores = L0_attn_x.shape[0]
    in_maps = []
    for b in range(ncores):
        xf = np.ascontiguousarray(np.asarray(L0_attn_x[b], dtype=np.float32).T).reshape(NK, 128, T)
        in_maps.append(dict(shared, x_fm=xf))
    return in_maps


def kernel(L0_attn_x, L0_attn_Wq, L0_attn_Wf, L0_attn_Wi, L0_attn_Wo,
           L0_attn_g_weight, L0_ffn_gate_w, L0_ffn_w_gate, L0_ffn_w_up,
           L0_ffn_w_down):
    T = L0_attn_x.shape[1]
    in_maps = _layout_inputs(L0_attn_x, L0_attn_Wq, L0_attn_Wf, L0_attn_Wi,
                             L0_attn_Wo, L0_attn_g_weight, L0_ffn_gate_w,
                             L0_ffn_w_gate, L0_ffn_w_up, L0_ffn_w_down, T=T)
    nc = build_program(T=T)
    res = run_bass_kernel_spmd(nc, in_maps, core_ids=list(range(len(in_maps))))
    outs = []
    for r in res.results:
        o = np.asarray(r["out_s"], np.float32)
        outs.append(o.transpose(2, 0, 1).reshape(T, H))
    return np.stack(outs, axis=0)


# version 14
# speedup vs baseline: 1.1804x; 1.1804x over previous
import math
import numpy as np
from contextlib import ExitStack

import concourse.bass as bass
import concourse.mybir as mybir
from concourse.bass_utils import run_bass_kernel_spmd

F32 = mybir.dt.float32
BF16 = mybir.dt.bfloat16
AF = mybir.ActivationFunctionType
ALU = mybir.AluOpType

H = 2048
NK = 16
NH = 16
DK = 128
FF = 8192
NF = FF // 128
NE = 2
EPS = 1e-5
SH = 50.0
LNC = math.log(1.0 / math.sqrt(DK))
ACL = -135.0
RBAND = 8


def _preload_set(nc, eng, set_name):
    from concourse.hw_specs import get_activation_tables
    for idx, name in enumerate(get_activation_tables(nc.m.arch).keys()):
        if name == set_name:
            eng.add_instruction(mybir.InstLoadActFuncSet(
                name=nc.get_next_instruction_name(), act_func_set_id=idx, ins=[], outs=[]))
            return
    raise ValueError(set_name)


def build_program(T=2048, scratch_kind="Internal", dbg=False, stop_after=None):
    CH = T // 128
    NSL = T // 512
    PT = min(T, 1024)
    NP = T // PT
    NS4 = PT // 512
    assert NS4 == 2 and T % 1024 == 0

    nc = bass.Bass("TRN2", target_bir_lowering=False)

    def dram(name, shape, dt=F32, kind="ExternalInput"):
        return nc.dram_tensor(name, shape, dt, kind=kind).ap()

    x_fm = dram("x_fm", [NK, 128, T])
    wq = dram("wq", [NK, 128, H])
    wf = dram("wf", [NK, 128, H])
    wi = dram("wi", [NK, 128, H])
    wo = dram("wo", [NK, 128, H])
    wgt = dram("wgt", [NE, NF, 128, H])
    wut = dram("wut", [NE, NF, 128, H])
    wdt = dram("wdt", [NE, NF, 128, H])
    grep = dram("grep", [NE, NK, 128, 128])
    gwt = dram("gwt", [128, NH])
    cones = dram("cones", [128, 128])
    cident = dram("cident", [128, 128])
    ctri = dram("ctri", [128, 256])
    cbias = dram("cbias", [128, 8])

    q_s = dram("q_s", [T // 128, 128, H], BF16, kind=scratch_kind)
    g_s = dram("g_s", [T // 128, 128, H], F32, kind=scratch_kind)
    k_s = dram("k_s", [T // 128, 128, H], BF16, kind=scratch_kind)
    v_s = dram("v_s", [T // 128, 128, H], BF16, kind=scratch_kind)
    o_s = dram("o_s", [NH, 128, T], BF16, kind=scratch_kind)
    x1_s = dram("x1_s", [NK, 128, T], F32, kind=scratch_kind)
    out_s = dram("out_s", [NK, 128, T], F32, kind="ExternalOutput")
    if dbg:
        d_mrep = dram("d_mrep", [128, T], F32, kind="ExternalOutput")
        d_rrep = dram("d_rrep", [128, T], F32, kind="ExternalOutput")
        d_ln1 = dram("d_ln1", [128, NK * T], BF16, kind="ExternalOutput")

    TOP = ExitStack()
    with TOP:
        E = TOP.enter_context

        def sem(name):
            return E(nc.semaphore(name))

        onesb = E(nc.sbuf_tensor("onesb", [128, 128], BF16))
        onesf = E(nc.sbuf_tensor("onesf", [128, 128], F32))
        identb = E(nc.sbuf_tensor("identb", [128, 128], BF16))
        trif = E(nc.sbuf_tensor("trif", [128, 256], BF16))
        gwtile = E(nc.sbuf_tensor("gwtile", [128, NH], F32))
        cbt = E(nc.sbuf_tensor("cbt", [128, 8], F32))
        mrepA = E(nc.sbuf_tensor("mrepA", [128, T], F32))
        rrepA = E(nc.sbuf_tensor("rrepA", [128, T], F32))

        def bcol(i):
            return cbt[:, i:i + 1]

        s_bar = sem("s_bar")
        bar_n = [0]

        def barrier():
            with nc.Block() as blk:
                tgt = bar_n[0] + 5
                def mk(e):
                    e.drain()
                    e.sem_inc(s_bar, 1)
                    e.wait_ge(s_bar, tgt)
                blk.tensor(mk)
                blk.vector(mk)
                blk.scalar(mk)
                blk.gpsimd(mk)
                blk.sync(mk)
                bar_n[0] = tgt

        s_cst = sem("s_cst")
        s_cpl = sem("s_cpl")
        s_x0 = sem("s_x0")
        s_sq = sem("s_sq")
        s_mst = sem("s_mst")
        s_v0 = sem("s_v0")
        s_ln1 = sem("s_ln1")
        s_wh = [sem("s_wh0"), sem("s_wh1")]
        s_mmP = sem("s_mmP")
        s_evP = sem("s_evP")
        s_stP = [sem("s_stP0"), sem("s_stP1"), sem("s_stP2")]
        s_ev1 = sem("s_ev1")
        s_ev2 = sem("s_ev2")

        with ExitStack() as ph01:
            E01 = ph01.enter_context
            xb = E01(nc.sbuf_tensor("xb", [128, NK * T], BF16))
            mrep = E01(nc.sbuf_tensor("mrep", [128, T], F32))
            rrep = E01(nc.sbuf_tensor("rrep", [128, T], F32))
            wbuf = [E01(nc.sbuf_tensor(f"wbuf{i}", [128, NK * 1024], BF16)) for i in range(2)]

            with ExitStack() as ph:
                Ep = ph.enter_context
                xsq = [Ep(nc.sbuf_tensor(f"xsq{i}", [128, T], BF16)) for i in range(2)]
                t0a = Ep(nc.sbuf_tensor("t0a", [128, T], F32))
                t0b = Ep(nc.sbuf_tensor("t0b", [128, T], F32))
                ssum = [Ep(nc.psum_tensor(f"ssum{i}", [128, 512], F32)) for i in range(NSL)]
                ssq = [Ep(nc.psum_tensor(f"ssq{i}", [128, 512], F32)) for i in range(NSL)]
                blk = Ep(nc.Block())

                @blk.gpsimd
                def _(g):
                    g.dma_start(out=onesb[:], in_=cones).then_inc(s_cst, 16)
                    g.dma_start(out=identb[:], in_=cident).then_inc(s_cst, 16)
                    g.dma_start(out=trif[:], in_=ctri).then_inc(s_cst, 16)
                    for k in range(NK):
                        g.dma_start(out=xb[:, k * T:(k + 1) * T], in_=x_fm[k]).then_inc(s_x0, 16)
                    for k in range(NK):
                        g.dma_start(out=wbuf[0][:, k * 1024:(k + 1) * 1024],
                                    in_=wq[k][:, 0:1024]).then_inc(s_wh[0], 16)

                @blk.sync
                def _(sy):
                    sy.dma_start(out=onesf[:], in_=cones).then_inc(s_cpl, 16)
                    sy.dma_start(out=gwtile[:], in_=gwt).then_inc(s_cpl, 16)
                    sy.dma_start(out=cbt[:], in_=cbias).then_inc(s_cpl, 16)

                @blk.scalar
                def _(s_):
                    _preload_set(nc, s_, "natural_log_exp_and_others")
                    s_.wait_ge(s_x0, 16 * NK)
                    for k in range(NK):
                        if k >= 2:
                            s_.wait_ge(s_mst, k - 1)
                        s_.activation(xsq[k % 2][:], xb[:, k * T:(k + 1) * T], AF.Square).then_inc(s_sq, 1)
                    s_.wait_ge(s_cpl, 48)
                    for ns in range(NSL):
                        s_.wait_ge(s_v0, ns + 1)
                        sl = slice(ns * 512, (ns + 1) * 512)
                        s_.activation(t0a[:, sl], t0b[:, sl], AF.Ln, bias=bcol(0))
                        s_.drain()
                        s_.activation(rrep[:, sl], t0a[:, sl], AF.Exp, scale=-0.5)

                @blk.tensor
                def _(t):
                    t.wait_ge(s_cst, 48)
                    t.wait_ge(s_x0, 16 * NK)
                    for k in range(NK):
                        t.wait_ge(s_sq, k + 1)
                        for ns in range(NSL):
                            sl = slice(ns * 512, (ns + 1) * 512)
                            t.matmul(ssum[ns][:], onesb[:], xb[:, k * T:(k + 1) * T][:, sl],
                                     start=(k == 0), stop=(k == NK - 1))
                            mm = t.matmul(ssq[ns][:], onesb[:], xsq[k % 2][:, sl],
                                          start=(k == 0), stop=(k == NK - 1))
                        mm.then_inc(s_mst, 1)

                @blk.vector
                def _(v):
                    for ns in range(NSL):
                        v.wait_ge(s_mst, NK)
                        sl = slice(ns * 512, (ns + 1) * 512)
                        v.tensor_scalar(mrep[:, sl], ssum[ns][:], 1.0 / H, None, ALU.mult)
                        v.drain()
                        v.tensor_tensor(t0a[:, sl], mrep[:, sl], mrep[:, sl], ALU.mult)
                        v.drain()
                        v.scalar_tensor_tensor(t0b[:, sl], ssq[ns][:], 1.0 / H, t0a[:, sl],
                                               ALU.mult, ALU.subtract).then_inc(s_v0, 1)

            barrier()

            NT = T // 512
            with ExitStack() as ph:
                Ep = ph.enter_context
                t1a = Ep(nc.sbuf_tensor("t1a", [128, T], F32))
                stq = [Ep(nc.sbuf_tensor(f"stq{i}", [128, 512], BF16)) for i in range(3)]
                stf = [Ep(nc.sbuf_tensor(f"stf{i}", [128, 512], F32)) for i in range(3)]
                e_ta = [Ep(nc.sbuf_tensor(f"e_ta{i}", [128, 512], F32)) for i in range(3)]
                e_tb = [Ep(nc.sbuf_tensor(f"e_tb{i}", [128, 512], F32)) for i in range(3)]
                e_tc = [Ep(nc.sbuf_tensor(f"e_tc{i}", [128, 512], F32)) for i in range(3)]
                e_td = Ep(nc.sbuf_tensor("e_td", [128, 512], F32))
                pp = [Ep(nc.psum_tensor(f"pp{i}", [128, 512], F32)) for i in range(6)]
                blk = Ep(nc.Block())

                groups = []
                for mat in range(3):
                    for hh in range(2):
                        hidx = mat * 2 + hh
                        if mat < 2:
                            for d8 in range(8):
                                for nt in range(NT):
                                    groups.append((mat, hidx, hh * 8 + d8, nt))
                        else:
                            for tb in range(T // 128):
                                for nt in range(2):
                                    groups.append((mat, hidx, tb, nt))
                wsrc = [wq, wf, wi]
                _fprefix = []
                _nf = 0
                for _g in groups:
                    if _g[0] == 1:
                        _nf += 1
                    _fprefix.append(_nf)

                def _fcount(gi):
                    return _fprefix[gi]
                gend = {}
                acc = 0
                for mat in range(3):
                    for hh in range(2):
                        hidx = mat * 2 + hh
                        acc += 8 * NT if mat < 2 else (T // 128) * 2
                        gend[hidx] = acc

                @blk.vector
                def _(v):
                    for k in range(NK):
                        xk = xb[:, k * T:(k + 1) * T]
                        v.tensor_tensor(t1a[:], xk, mrep[:], ALU.subtract)
                        v.drain()
                        v.tensor_tensor(xk, t1a[:], rrep[:], ALU.mult).then_inc(s_ln1, 1)
                        v.drain()
                    nf = 0
                    for gi, (mat, hidx, dd, nt) in enumerate(groups):
                        if mat != 1:
                            continue
                        nf += 1
                        v.wait_ge(s_ev1, nf)
                        v.tensor_tensor(e_td[:], pp[gi % 6][:], e_ta[gi % 3][:], ALU.subtract)
                        v.drain()
                        v.scalar_tensor_tensor(stf[gi % 3][:], e_td[:], 0.5, e_tc[gi % 3][:],
                                               ALU.mult, ALU.subtract).then_inc(s_ev2, 1)
                        v.drain()

                @blk.gpsimd
                def _(g):
                    for mat in range(3):
                        for hh in range(2):
                            hidx = mat * 2 + hh
                            if hidx == 0:
                                continue
                            if hidx >= 2:
                                g.wait_ge(s_mmP, gend[hidx - 2])
                            for k in range(NK):
                                g.dma_start(out=wbuf[hidx % 2][:, k * 1024:(k + 1) * 1024],
                                            in_=wsrc[mat][k][:, hh * 1024:(hh + 1) * 1024]
                                            ).then_inc(s_wh[hidx % 2], 16)

                @blk.tensor
                def _(t):
                    for gi, (mat, hidx, dd, nt) in enumerate(groups):
                        if gi == 2:
                            t.wait_ge(s_ln1, NK)
                        if gi == 0 or groups[gi - 1][1] != hidx:
                            t.wait_ge(s_wh[hidx % 2], 16 * NK * (hidx // 2 + 1))
                        if gi >= 6:
                            t.wait_ge(s_evP, gi - 5)
                        wb = wbuf[hidx % 2]
                        if mat < 2:
                            dlo = (dd % 8) * 128
                            for k in range(NK):
                                if gi < 2:
                                    t.wait_ge(s_ln1, k + 1)
                                mm = t.matmul(pp[gi % 6][:],
                                              wb[:, k * 1024 + dlo:k * 1024 + dlo + 128],
                                              xb[:, k * T + nt * 512: k * T + nt * 512 + 512],
                                              start=(k == 0), stop=(k == NK - 1))
                        else:
                            for k in range(NK):
                                mm = t.matmul(pp[gi % 6][:],
                                              xb[:, k * T + dd * 128: k * T + dd * 128 + 128],
                                              wb[:, k * 1024 + nt * 512: k * 1024 + nt * 512 + 512],
                                              start=(k == 0), stop=(k == NK - 1))
                        mm.then_inc(s_mmP, 1)

                cum = {0: [0] * len(groups), 1: [0] * len(groups), 2: [0] * len(groups)}
                tot = [0, 0, 0]
                for gi, (mat, hidx, dd, nt) in enumerate(groups):
                    tot[gi % 3] += 32 if mat == 1 else 16
                    for par in range(3):
                        cum[par][gi] = tot[par]

                def _cm(tensor_s, dd, nt):
                    return tensor_s[4 * nt:4 * nt + 4, :, dd * 128:(dd + 1) * 128].rearrange(
                        "c p t -> p c t")

                @blk.scalar
                def _(s_):
                    s_.preload_act_table(AF.Silu)
                    for gi, (mat, hidx, dd, nt) in enumerate(groups):
                        if gi > 0 and groups[gi - 1][0] == 0 and mat == 1:
                            _preload_set(nc, s_, "natural_log_exp_and_others")
                        s_.wait_ge(s_mmP, gi + 1)
                        if gi >= 3:
                            s_.wait_ge(s_stP[gi % 3], cum[gi % 3][gi - 3])
                        if mat == 0:
                            s_.activation(stq[gi % 3][:], pp[gi % 6][:], AF.Silu).then_inc(s_evP, 1)
                        elif mat == 1:
                            s_.activation(e_ta[gi % 3][:], pp[gi % 6][:], AF.Abs)
                            s_.drain()
                            s_.activation(e_tb[gi % 3][:], e_ta[gi % 3][:], AF.Exp, scale=-1.0)
                            s_.drain()
                            s_.activation(e_tc[gi % 3][:], e_tb[gi % 3][:], AF.Ln,
                                          bias=bcol(1)).then_inc(s_ev1, 1)
                            s_.wait_ge(s_ev2, _fcount(gi))
                            s_.activation(e_tb[gi % 3][:], stf[gi % 3][:], AF.Exp)
                            s_.drain()
                            s_.activation(stq[gi % 3][:], e_tb[gi % 3][:], AF.Copy,
                                          scale=-1.0, bias=1.0).then_inc(s_evP, 1)
                        else:
                            s_.copy(stq[gi % 3][:], pp[gi % 6][:]).then_inc(s_evP, 1)

                @blk.sync
                def _(sy):
                    if dbg:
                        sy.wait_ge(s_ln1, NK)
                        sy.dma_start(out=d_mrep, in_=mrep[:]).then_inc(s_cpl, 16)
                        sy.dma_start(out=d_rrep, in_=rrep[:]).then_inc(s_cpl, 16)
                        sy.dma_start(out=d_ln1, in_=xb[:]).then_inc(s_cpl, 16)
                        sy.wait_ge(s_cpl, 64 + 48)
                    nstores = [0, 0, 0]
                    for gi, (mat, hidx, dd, nt) in enumerate(groups):
                        sy.wait_ge(s_evP, gi + 1)
                        if mat == 0:
                            sy.dma_start(out=_cm(q_s, dd, nt),
                                         in_=stq[gi % 3][:]).then_inc(s_stP[gi % 3], 16)
                            nstores[gi % 3] += 16
                        elif mat == 1:
                            sy.dma_start(out=_cm(g_s, dd, nt),
                                         in_=stf[gi % 3][:]).then_inc(s_stP[gi % 3], 16)
                            sy.dma_start(out=_cm(k_s, dd, nt),
                                         in_=stq[gi % 3][:]).then_inc(s_stP[gi % 3], 16)
                            nstores[gi % 3] += 32
                        else:
                            hh2 = hidx - 4
                            dst = v_s[dd][:, hh2 * 1024 + nt * 512: hh2 * 1024 + (nt + 1) * 512]
                            sy.dma_start(out=dst, in_=stq[gi % 3][:]).then_inc(s_stP[gi % 3], 16)
                            nstores[gi % 3] += 16
                    sy.wait_ge(s_stP[0], nstores[0])
                    sy.wait_ge(s_stP[1], nstores[1])
                    sy.wait_ge(s_stP[2], nstores[2])

        barrier()
        if stop_after == 1:
            return nc

        s_sl = [sem("s_sl0"), sem("s_sl1")]
        s_slg = sem("s_slg")
        s_slk = sem("s_slk")
        s_slq = sem("s_slq")
        s_ae = sem("s_ae")
        s_ke = sem("s_ke")
        s_kd = sem("s_kd")
        s_vA = sem("s_vA")
        s_s1 = sem("s_s1")
        s_m1 = sem("s_m1")
        s_s2 = sem("s_s2")
        s_o1 = sem("s_o1")
        s_s3 = sem("s_s3")
        s_m3 = sem("s_m3")
        s_s4 = sem("s_s4")
        s_h4 = sem("s_h4")
        s_ost = sem("s_ost")
        s_pz = sem("s_pz")

        with ExitStack() as ph:
            Ep = ph.enter_context
            qsl = Ep(nc.sbuf_tensor("qsl", [128, 4 * H], BF16))
            vsl = [Ep(nc.sbuf_tensor(f"vsl{i}", [128, 4 * H], BF16)) for i in range(2)]
            gsl = Ep(nc.sbuf_tensor("gsl", [128, 4 * H], F32))
            ksl = Ep(nc.sbuf_tensor("ksl", [128, 4 * H], BF16))
            osl = Ep(nc.sbuf_tensor("osl", [128, NH * 512], BF16))
            hst = Ep(nc.sbuf_tensor("hst", [128, NH * 128], F32))
            hbf = Ep(nc.sbuf_tensor("hbf", [128, NH * 128], BF16))
            AA = [Ep(nc.sbuf_tensor(f"AA{i}", [128, NH * 128], F32)) for i in range(2)]
            e1t = Ep(nc.sbuf_tensor("e1t", [128, NH * 128], F32))
            e2t = Ep(nc.sbuf_tensor("e2t", [128, NH * 128], F32))
            qe = Ep(nc.sbuf_tensor("qe", [128, NH * 128], BF16))
            ke = Ep(nc.sbuf_tensor("ke", [128, NH * 128], BF16))
            kd = Ep(nc.sbuf_tensor("kd", [128, NH * 128], BF16))
            ssb = Ep(nc.sbuf_tensor("ssb", [128, NH * 128], BF16))
            ktm = Ep(nc.sbuf_tensor("ktm", [128, NH * 128], BF16))
            ec50 = Ep(nc.sbuf_tensor("ec50", [128, NH], F32))
            ecC = Ep(nc.sbuf_tensor("ecC", [128, NH], F32))
            sp = [Ep(nc.psum_tensor(f"sp{i}", [128, 256], F32)) for i in range(2)]
            op = [Ep(nc.psum_tensor(f"op{i}", [128, 256], F32)) for i in range(2)]
            tp = [Ep(nc.psum_tensor(f"tp{i}", [128, 512], BF16)) for i in range(2)]
            hp = [Ep(nc.psum_tensor(f"hp{i}", [128, 256], F32)) for i in range(2)]
            blk = Ep(nc.Block())

            def hsl(tile, hd, w=128):
                return tile[:, hd * w:(hd + 1) * w]

            def chv(tile, cc, hd):
                return tile[:, cc * H + hd * 128: cc * H + (hd + 1) * 128]

            @blk.sync
            def _(sy):
                def load_qv(s):
                    sy.dma_start(out=vsl[s % 2][:].rearrange("p (c t) -> p c t", c=4),
                                 in_=v_s[s * 4:(s + 1) * 4].rearrange("c p t -> p c t")
                                 ).then_inc(s_sl[s % 2], 16)

                def load_q(s):
                    sy.dma_start(out=qsl[:].rearrange("p (c t) -> p c t", c=4),
                                 in_=q_s[s * 4:(s + 1) * 4].rearrange("c p t -> p c t")
                                 ).then_inc(s_slq, 16)

                def load_g(s):
                    sy.dma_start(out=gsl[:].rearrange("p (c t) -> p c t", c=4),
                                 in_=g_s[s * 4:(s + 1) * 4].rearrange("c p t -> p c t")
                                 ).then_inc(s_slg, 16)

                def load_k(s):
                    sy.dma_start(out=ksl[:].rearrange("p (c t) -> p c t", c=4),
                                 in_=k_s[s * 4:(s + 1) * 4].rearrange("c p t -> p c t")
                                 ).then_inc(s_slk, 16)

                for s in range(min(2, NSL)):
                    load_qv(s)
                load_g(0)
                load_k(0)
                load_q(0)
                for s in range(NSL):
                    sy.wait_ge(s_o1, 32 * (s + 1))
                    t0 = s * 512
                    sy.dma_start(out=o_s[:, :, t0:t0 + 512].rearrange("h p t -> p h t"),
                                 in_=osl[:].rearrange("p (h t) -> p h t", h=NH)
                                 ).then_inc(s_ost, 16)
                    if s + 1 < NSL:
                        sy.wait_ge(s_vA, 4 * (s + 1))
                        load_g(s + 1)
                        sy.wait_ge(s_ke, 4 * (s + 1))
                        load_k(s + 1)
                        load_q(s + 1)
                    if s + 2 < NSL:
                        sy.wait_ge(s_h4, 64 * (s + 1))
                        load_qv(s + 2)
                sy.wait_ge(s_ost, 16 * NSL)

            @blk.scalar
            def _(s_):
                _preload_set(nc, s_, "natural_log_exp_and_others")
                for ci in range(CH):
                    s = ci // 4
                    cc = ci % 4
                    if ci >= 1:
                        s_.wait_ge(s_ke, ci)
                    s_.wait_ge(s_vA, ci + 1)
                    s_.activation(e2t[:], AA[ci % 2][:], AF.Exp, scale=-1.0, bias=bcol(3))
                    s_.drain()
                    s_.activation(e1t[:], AA[ci % 2][:], AF.Exp, scale=1.0, bias=bcol(2))
                    s_.drain()
                    aac = AA[ci % 2][:].rearrange("p (h t) -> p h t", h=NH)[:, :, 127:128]
                    s_.activation(ec50[:].rearrange("p (h o) -> p h o", o=1), aac,
                                  AF.Exp, bias=bcol(4)).then_inc(s_ae, 1)
                    if ci >= 1:
                        s_.wait_ge(s_s3, 4 * ci)
                    s_.wait_ge(s_ke, ci + 1)
                    for hd in range(NH):
                        s_.activation(hsl(kd, hd), hsl(ke, hd), AF.Copy,
                                      scale=ec50[:, hd:hd + 1])
                    s_.wait_ge(s_h4, 16 * ci)
                    s_.activation(hbf[:], hst[:], AF.Copy,
                                  scale=math.exp(-SH)).then_inc(s_kd, 1)
                    if s >= 1 and cc == 0:
                        s_.wait_ge(s_ost, 16 * s)
                    for pr in range(NH // 2):
                        s_.wait_ge(s_s2, 8 * ci + pr + 1)
                        dst = osl[:].rearrange("p (h t) -> p h t", h=NH)[
                            :, 2 * pr:2 * pr + 2, cc * 128:(cc + 1) * 128]
                        s_.copy(dst, op[pr % 2][:].rearrange("p (h t) -> p h t", h=2)
                                ).then_inc(s_o1, 1)

            @blk.vector
            def _(v):
                v.memset(hst[:], 0.0)
                v.memset(sp[0][:], 0.0)
                v.memset(sp[1][:], 0.0)
                v.drain()
                v.engine_nop().then_inc(s_pz, 1)
                for ci in range(CH):
                    s = ci // 4
                    cc = ci % 4
                    v.wait_ge(s_slg, 16 * (s + 1))
                    if ci >= 2:
                        v.wait_ge(s_ae, ci - 1)
                    for hd in range(NH):
                        v.tensor_tensor_scan(hsl(AA[ci % 2], hd), onesf[:], chv(gsl, cc, hd),
                                             0.0, ALU.mult, ALU.add)
                    v.drain()
                    v.tensor_single_scalar(AA[ci % 2][:], AA[ci % 2][:], ACL,
                                           ALU.max).then_inc(s_vA, 1)
                    v.wait_ge(s_slk, 16 * (s + 1))
                    v.wait_ge(s_slq, 16 * (s + 1))
                    if ci >= 1:
                        v.wait_ge(s_s2, 8 * ci)
                        v.wait_ge(s_kd, ci)
                    v.wait_ge(s_ae, ci + 1)
                    v.tensor_scalar(ecC[:], ec50[:], math.exp(-SH), None, ALU.mult)
                    v.tensor_tensor(ke[:], ksl[:, cc * H:(cc + 1) * H], e2t[:], ALU.mult)
                    v.tensor_tensor(qe[:], qsl[:, cc * H:(cc + 1) * H],
                                    e1t[:], ALU.mult).then_inc(s_ke, 1)
                    for pr in range(NH // 2):
                        v.wait_ge(s_s1, 8 * ci + pr + 1)
                        v.tensor_tensor(ssb[:, pr * 256:(pr + 1) * 256], sp[pr % 2][:], trif[:],
                                        ALU.mult).then_inc(s_m1, 1)
                    for qd in range(NH // 4):
                        v.wait_ge(s_s3, 4 * ci + qd + 1)
                        v.tensor_copy(ktm[:, qd * 512:(qd + 1) * 512],
                                      tp[qd % 2][:]).then_inc(s_m3, 1)
                    for hd in range(NH):
                        v.wait_ge(s_s4, 8 * ci + hd // 2 + 1)
                        v.scalar_tensor_tensor(hsl(hst, hd), hsl(hst, hd), ecC[:, hd:hd + 1],
                                               hp[(hd // 2) % 2][:, (hd % 2) * 128:(hd % 2 + 1) * 128],
                                               ALU.mult, ALU.add).then_inc(s_h4, 1)

            @blk.tensor
            def _(t):
                for ci in range(CH):
                    s = ci // 4
                    cc = ci % 4
                    t.wait_ge(s_ke, ci + 1)
                    t.wait_ge(s_sl[s % 2], 16 * (s // 2 + 1))
                    if ci == 0:
                        t.wait_ge(s_pz, 1)
                    for pr in range(NH // 2):
                        if 8 * ci + pr >= 2:
                            t.wait_ge(s_m1, 8 * ci + pr - 1)
                        for q2 in range(2):
                            hd = 2 * pr + q2
                            reg = sp[pr % 2][:, q2 * 128:(q2 + 1) * 128]
                            t.matmul(reg[0:64, :], hsl(ke, hd)[:, 0:64], hsl(qe, hd),
                                     start=True, stop=True)
                            mmq = t.matmul(reg[64:128, 64:128], hsl(ke, hd)[:, 64:128],
                                           hsl(qe, hd)[:, 64:128], start=True, stop=True)
                        mmq.then_inc(s_s1, 1)
                    t.wait_ge(s_kd, ci + 1)
                    for pr in range(NH // 2):
                        t.wait_ge(s_m1, 8 * ci + pr + 1)
                        if 8 * ci + pr >= 2:
                            t.wait_ge(s_o1, 8 * ci + pr - 1)
                        for q2 in range(2):
                            hd = 2 * pr + q2
                            reg = op[pr % 2][:, q2 * 128:(q2 + 1) * 128]
                            vv = chv(vsl[s % 2], cc, hd)
                            t.matmul(reg, vv, hsl(ssb, hd), start=True, stop=False)
                            mmq = t.matmul(reg, hsl(hbf, hd), hsl(qe, hd),
                                           start=False, stop=True)
                        mmq.then_inc(s_s2, 1)
                    for qd in range(NH // 4):
                        if 4 * ci + qd >= 2:
                            t.wait_ge(s_m3, 4 * ci + qd - 1)
                        for q4 in range(4):
                            hd = 4 * qd + q4
                            mmq = t.transpose(tp[qd % 2][:, q4 * 128:(q4 + 1) * 128],
                                              hsl(kd, hd), identb[:])
                        mmq.then_inc(s_s3, 1)
                    for pr in range(NH // 2):
                        t.wait_ge(s_m3, 4 * ci + pr // 2 + 1)
                        if 8 * ci + pr >= 2:
                            t.wait_ge(s_h4, 2 * (8 * ci + pr) - 2)
                        for q2 in range(2):
                            hd = 2 * pr + q2
                            vv = chv(vsl[s % 2], cc, hd)
                            mmq = t.matmul(hp[pr % 2][:, q2 * 128:(q2 + 1) * 128],
                                           hsl(ktm, hd), vv, start=True, stop=True)
                        mmq.then_inc(s_s4, 1)

        barrier()
        if stop_after == 2:
            return nc

        s_wo = sem("s_wo")
        s_sl3 = [sem("s_sl30"), sem("s_sl31")]
        s_slo3 = sem("s_slo3")
        s_sq3 = sem("s_sq3")
        s_st3 = sem("s_st3")
        s_rs3 = sem("s_rs3")
        s_oh3 = sem("s_oh3")
        s_mm3 = sem("s_mm3")
        s_ev3 = sem("s_ev3")
        s_x1st = [sem("s_x1st0"), sem("s_x1st1")]
        s_sq5 = sem("s_sq5")
        s_st4 = sem("s_st4")
        s_v5 = sem("s_v5")

        with ExitStack() as ph:
            Ep = ph.enter_context
            wob = Ep(nc.sbuf_tensor("wob", [128, NK * H], BF16))
            osl2s = Ep(nc.sbuf_tensor("osl2s", [128, NH * 512], BF16))
            xsl = [Ep(nc.sbuf_tensor(f"xsl{i}", [128, NK * 512], F32)) for i in range(2)]
            osq = Ep(nc.sbuf_tensor("osq", [128, NH * 512], BF16))
            ohat = Ep(nc.sbuf_tensor("ohat", [128, NH * 512], BF16))
            rstd3 = Ep(nc.sbuf_tensor("rstd3", [128, 512], F32))
            stg3 = [Ep(nc.sbuf_tensor(f"stg3{i}", [128, 512], F32)) for i in range(2)]
            xb3 = [Ep(nc.sbuf_tensor(f"xb3{i}", [128, 512], BF16)) for i in range(2)]
            xq3 = [Ep(nc.sbuf_tensor(f"xq3{i}", [128, 512], BF16)) for i in range(2)]
            t3b = Ep(nc.sbuf_tensor("t3b", [128, 512], F32))
            st3 = [Ep(nc.psum_tensor(f"st3{i}", [128, 512], F32)) for i in range(2)]
            pw = [Ep(nc.psum_tensor(f"pw{i}", [128, 512], F32)) for i in range(2)]
            st4s = Ep(nc.psum_tensor("st4s", [128, 512], F32))
            st4q = Ep(nc.psum_tensor("st4q", [128, 512], F32))
            blk = Ep(nc.Block())

            @blk.gpsimd
            def _(g):
                for k in range(NK):
                    g.dma_start(out=wob[:, k * H:(k + 1) * H], in_=wo[k]).then_inc(s_wo, 16)

            @blk.sync
            def _(sy):
                def load_o(s):
                    t0 = s * 512
                    sy.dma_start(out=osl2s[:].rearrange("p (h t) -> p h t", h=NH),
                                 in_=o_s[:, :, t0:t0 + 512].rearrange("h p t -> p h t")).then_inc(s_slo3, 16)
                def load_x(s):
                    t0 = s * 512
                    sy.dma_start(out=xsl[s % 2][:].rearrange("p (k t) -> p k t", k=NK),
                                 in_=x_fm[:, :, t0:t0 + 512].rearrange("k p t -> p k t")).then_inc(s_sl3[s % 2], 16)
                load_o(0)
                for s in range(min(2, NSL)):
                    load_x(s)
                nst = [0, 0]
                for s in range(NSL):
                    t0 = s * 512
                    if s + 1 < NSL:
                        sy.wait_ge(s_oh3, s + 1)
                        load_o(s + 1)
                    for d in range(NK):
                        gd = NK * s + d
                        sy.wait_ge(s_ev3, gd + 1)
                        sy.dma_start(out=x1_s[d][:, t0:t0 + 512],
                                     in_=stg3[gd % 2][:]).then_inc(s_x1st[gd % 2], 16)
                        nst[gd % 2] += 16
                    if s + 2 < NSL:
                        load_x(s + 2)
                sy.wait_ge(s_x1st[0], nst[0])
                sy.wait_ge(s_x1st[1], nst[1])

            @blk.scalar
            def _(s_):
                _preload_set(nc, s_, "natural_log_exp_and_others")
                for s in range(NSL):
                    s_.wait_ge(s_slo3, 16 * (s + 1))
                    s_.activation(osq[:], osl2s[:], AF.Square).then_inc(s_sq3, 1)
                    s_.wait_ge(s_st3, s + 1)
                    s_.activation(rstd3[:], st3[s % 2][:], AF.Ln, scale=1.0 / H, bias=bcol(0))
                    s_.drain()
                    s_.activation(rstd3[:], rstd3[:], AF.Exp, scale=-0.5).then_inc(s_rs3, 1)
                    for d in range(NK):
                        gd = NK * s + d
                        s_.wait_ge(s_ev3, gd + 1)
                        if gd >= 2:
                            s_.wait_ge(s_st4, gd - 1)
                        s_.copy(xb3[gd % 2][:], stg3[gd % 2][:])
                        s_.drain()
                        s_.activation(xq3[gd % 2][:], stg3[gd % 2][:], AF.Square).then_inc(s_sq5, 1)
                    s_.wait_ge(s_v5, s + 1)
                    sl5 = slice(s * 512, (s + 1) * 512)
                    s_.activation(t3b[:], t3b[:], AF.Ln, bias=bcol(0))
                    s_.drain()
                    s_.activation(rrepA[:, sl5], t3b[:], AF.Exp, scale=-0.5)

            @blk.vector
            def _(v):
                for s in range(NSL):
                    v.wait_ge(s_rs3, s + 1)
                    v.wait_ge(s_sl3[s % 2], 16 * (s // 2 + 1))
                    for hd in range(NH):
                        a = v.scalar_tensor_tensor(ohat[:, hd * 512:(hd + 1) * 512],
                                                   osl2s[:, hd * 512:(hd + 1) * 512],
                                                   gwtile[:, hd:hd + 1], rstd3[:],
                                                   ALU.mult, ALU.mult)
                    a.then_inc(s_oh3, 1)
                    for d in range(NK):
                        gd = NK * s + d
                        v.wait_ge(s_mm3, gd + 1)
                        if gd >= 2:
                            v.wait_ge(s_x1st[gd % 2], 16 * (gd // 2))
                            v.wait_ge(s_sq5, gd - 1)
                        v.tensor_tensor(stg3[gd % 2][:], pw[d % 2][:],
                                        xsl[s % 2][:, d * 512:(d + 1) * 512], ALU.add).then_inc(s_ev3, 1)
                    v.wait_ge(s_st4, NK * (s + 1))
                    sl5 = slice(s * 512, (s + 1) * 512)
                    v.tensor_scalar(mrepA[:, sl5], st4s[:], 1.0 / H, None, ALU.mult)
                    v.drain()
                    v.tensor_tensor(t3b[:], mrepA[:, sl5], mrepA[:, sl5], ALU.mult)
                    v.drain()
                    v.scalar_tensor_tensor(t3b[:], st4q[:], 1.0 / H, t3b[:],
                                           ALU.mult, ALU.subtract).then_inc(s_v5, 1)
                    v.drain()

            @blk.tensor
            def _(t):
                t.wait_ge(s_wo, 16 * NK)
                for s in range(NSL):
                    t.wait_ge(s_sq3, s + 1)
                    if s >= 2:
                        t.wait_ge(s_rs3, s - 1)
                        t.wait_ge(s_ev3, NK * (s - 1))
                    for hd in range(NH):
                        mm = t.matmul(st3[s % 2][:], onesb[:], osq[:, hd * 512:(hd + 1) * 512],
                                      start=(hd == 0), stop=(hd == NH - 1))
                    mm.then_inc(s_st3, 1)
                    t.wait_ge(s_oh3, s + 1)
                    for d in range(NK):
                        gd = NK * s + d
                        if gd >= 2:
                            t.wait_ge(s_ev3, gd - 1)
                        for k in range(NK):
                            mm = t.matmul(pw[d % 2][:], wob[:, k * H + d * 128: k * H + d * 128 + 128],
                                          ohat[:, k * 512:(k + 1) * 512],
                                          start=(k == 0), stop=(k == NK - 1))
                        mm.then_inc(s_mm3, 1)
                        if d >= 2:
                            gd2 = gd - 2
                            t.wait_ge(s_sq5, gd2 + 1)
                            t.matmul(st4s[:], onesb[:], xb3[gd2 % 2][:],
                                     start=(d == 2), stop=False)
                            mm = t.matmul(st4q[:], onesb[:], xq3[gd2 % 2][:],
                                          start=(d == 2), stop=False)
                            mm.then_inc(s_st4, 1)
                    for dd2 in range(NK - 2, NK):
                        gd2 = NK * s + dd2
                        t.wait_ge(s_sq5, gd2 + 1)
                        t.matmul(st4s[:], onesb[:], xb3[gd2 % 2][:],
                                 start=False, stop=(dd2 == NK - 1))
                        mm = t.matmul(st4q[:], onesb[:], xq3[gd2 % 2][:],
                                      start=False, stop=(dd2 == NK - 1))
                        mm.then_inc(s_st4, 1)

        barrier()
        if stop_after == 3:
            return nc

        s_yi = sem("s_yi")
        s_r4 = sem("s_r4")
        s_l4 = sem("s_l4")
        s_gl = sem("s_gl")
        s_dg = sem("s_dg")
        s_pd = sem("s_pd")
        s_ln2 = sem("s_ln2")
        s_w4 = [sem("s_w40"), sem("s_w41")]
        s_wd4 = sem("s_wd4")
        s_mmA = sem("s_mmA")
        s_evS = sem("s_evS")
        s_evA = sem("s_evA")
        s_mmB = sem("s_mmB")
        s_evB = sem("s_evB")
        s_out = sem("s_out")

        cnt = dict(yi=0, r4=0, l4=0, gl=0, dg=0, pd=0, ln2=0,
                   w4=[0, 0], wd4=0, mmA=0, evS=0, evA=0, mmB=0, evB=0, out=0)
        NB4 = NE * NF // RBAND

        with ExitStack() as ph4:
            E4 = ph4.enter_context
            Y_acc = E4(nc.sbuf_tensor("Y_acc", [128, NK * PT], F32))
            ln2 = E4(nc.sbuf_tensor("ln2", [128, NK * PT], BF16))
            w0rep = E4(nc.sbuf_tensor("w0rep", [128, PT], BF16))
            w1rep = E4(nc.sbuf_tensor("w1rep", [128, PT], BF16))
            hb4 = [E4(nc.sbuf_tensor(f"hb4_{i}", [128, RBAND * PT], BF16)) for i in range(2)]
            wgb = [E4(nc.sbuf_tensor(f"wgb{i}", [128, H], BF16)) for i in range(2)]
            wub = [E4(nc.sbuf_tensor(f"wub{i}", [128, H], BF16)) for i in range(2)]
            wdb = E4(nc.sbuf_tensor("wdb", [128, RBAND * H], BF16))
            tmp4 = E4(nc.sbuf_tensor("tmp4", [128, 2 * 512], BF16))
            t4a = E4(nc.sbuf_tensor("t4a", [128, PT], F32))
            dgb = E4(nc.sbuf_tensor("dgb", [128, NK * 128], BF16))
            g0b = hb4[0][:, 0:NK * 128]
            g1b = hb4[0][:, NK * 128:2 * NK * 128]
            gps = [E4(nc.psum_tensor(f"gps{i}", [128, 512], F32)) for i in range(2)]
            ups = [E4(nc.psum_tensor(f"ups{i}", [128, 512], F32)) for i in range(2)]
            yps = [E4(nc.psum_tensor(f"yps{i}", [128, 512], F32)) for i in range(2)]

            chunks = [(e, f) for e in range(NE) for f in range(NF)]
            wrep = [w0rep, w1rep]

            for p in range(NP):
                t0p = p * PT
                with ExitStack() as ph:
                    Ep = ph.enter_context
                    blk = Ep(nc.Block())
                    base = dict((k2, (list(v2) if isinstance(v2, list) else v2))
                                for k2, v2 in cnt.items())
                    d4 = rrepA[:, t0p:t0p + PT]

                    @blk.sync
                    def _(sy):
                        if p == 0:
                            for k in range(NK):
                                sy.dma_start(out=Y_acc[:, k * PT:(k + 1) * PT],
                                             in_=x1_s[k][:, t0p:t0p + PT]).then_inc(s_yi, 16)
                        for d in range(NK):
                            sy.wait_ge(s_evB, base["evB"] + 32 * (NB4 - 1) + 2 * d + 2)
                            sy.dma_start(out=out_s[d][:, t0p:t0p + PT],
                                         in_=Y_acc[:, d * PT:(d + 1) * PT]).then_inc(s_out, 16)
                            if p + 1 < NP:
                                sy.wait_ge(s_out, base["out"] + 16 * (d + 1))
                                sy.dma_start(out=Y_acc[:, d * PT:(d + 1) * PT],
                                             in_=x1_s[d][:, t0p + PT:t0p + 2 * PT]).then_inc(s_yi, 16)
                        sy.wait_ge(s_out, base["out"] + 16 * NK)

                    @blk.gpsimd
                    def _(g):
                        if p == 0:
                            g.dma_start(out=g0b.rearrange("p (k j) -> p k j", k=NK),
                                        in_=grep[0].rearrange("k p j -> p k j")).then_inc(s_gl, 16)
                            g.dma_start(out=g1b.rearrange("p (k j) -> p k j", k=NK),
                                        in_=grep[1].rearrange("k p j -> p k j")).then_inc(s_gl, 16)
                        for b in range(NB4):
                            for ii in range(RBAND):
                                fc = b * RBAND + ii
                                e, f = chunks[fc]
                                if fc >= 2:
                                    g.wait_ge(s_mmA, base["mmA"] + 2 * fc - 2)
                                g.dma_start(out=wgb[fc % 2][:], in_=wgt[e, f]).then_inc(s_w4[fc % 2], 16)
                                g.dma_start(out=wub[fc % 2][:], in_=wut[e, f]).then_inc(s_w4[fc % 2], 16)
                            if b >= 1:
                                g.wait_ge(s_mmB, base["mmB"] + 32 * b)
                            for ii in range(RBAND):
                                e, f = chunks[b * RBAND + ii]
                                g.dma_start(out=wdb[:, ii * H:(ii + 1) * H],
                                            in_=wdt[e, f]).then_inc(s_wd4, 16)


                    @blk.scalar
                    def _(s_):
                        _preload_set(nc, s_, "natural_log_exp_and_others")
                        for ns in range(NS4):
                            s_.wait_ge(s_l4, base["l4"] + ns + 1)
                            sl = slice(ns * 512, (ns + 1) * 512)
                            s_.activation(t4a[:, sl], d4[:, sl], AF.Abs)
                            s_.drain()
                            s_.activation(t4a[:, sl], t4a[:, sl], AF.Exp, scale=-1.0).then_inc(s_r4, 1)
                        _preload_set(nc, s_, "silu_and_others")
                        for j in range(2 * NE * NF):
                            s_.wait_ge(s_mmA, base["mmA"] + j + 1)
                            if j >= 2:
                                s_.wait_ge(s_evA, base["evA"] + j - 1)
                            s_.activation(tmp4[:, (j % 2) * 512:(j % 2 + 1) * 512], gps[j % 2][:],
                                          AF.Silu).then_inc(s_evS, 1)

                    @blk.tensor
                    def _(t):
                        def b_group(bb, gg2):
                            gl = 32 * bb + gg2
                            d, sx = gg2 // 2, gg2 % 2
                            if gl >= 2:
                                t.wait_ge(s_evB, base["evB"] + gl - 1)
                            for ii in range(RBAND):
                                mm = t.matmul(yps[gl % 2][:], wdb[:, ii * H + d * 128: ii * H + (d + 1) * 128],
                                              hb4[bb % 2][:, ii * PT + sx * 512: ii * PT + (sx + 1) * 512],
                                              start=(ii == 0), stop=(ii == RBAND - 1))
                            mm.then_inc(s_mmB, 1)
                        for b in range(NB4):
                            for jj in range(2 * RBAND):
                                j = 2 * RBAND * b + jj
                                fc, sx = j // 2, j % 2
                                if sx == 0:
                                    t.wait_ge(s_w4[fc % 2], base["w4"][fc % 2] + 32 * (fc // 2 + 1))
                                if j >= 2:
                                    t.wait_ge(s_evA, base["evA"] + j - 1)
                                    t.wait_ge(s_evS, base["evS"] + j - 1)
                                for k in range(NK):
                                    if j < 2:
                                        t.wait_ge(s_ln2, base["ln2"] + k + 1)
                                    t.matmul(gps[j % 2][:], wgb[fc % 2][:, k * 128:(k + 1) * 128],
                                             ln2[:, k * PT + sx * 512: k * PT + (sx + 1) * 512],
                                             start=(k == 0), stop=(k == NK - 1))
                                for k in range(NK):
                                    mm = t.matmul(ups[j % 2][:], wub[fc % 2][:, k * 128:(k + 1) * 128],
                                                  ln2[:, k * PT + sx * 512: k * PT + (sx + 1) * 512],
                                                  start=(k == 0), stop=(k == NK - 1))
                                mm.then_inc(s_mmA, 1)
                                if b == 0 and jj == 1:
                                    t.wait_ge(s_dg, base["dg"] + 1)
                                    t.wait_ge(s_ln2, base["ln2"] + NK)
                                    if p >= 1:
                                        t.wait_ge(s_evB, base["evB"])
                                    for ns in range(NS4):
                                        sl = slice(ns * 512, (ns + 1) * 512)
                                        for k in range(NK):
                                            mm = t.matmul(yps[ns][:], dgb[:][:, k * 128:(k + 1) * 128],
                                                          ln2[:, k * PT:(k + 1) * PT][:, sl],
                                                          start=(k == 0), stop=(k == NK - 1))
                                        mm.then_inc(s_pd, 1)
                            t.wait_ge(s_evA, base["evA"] + 2 * RBAND * (b + 1))
                            t.wait_ge(s_wd4, base["wd4"] + 16 * RBAND * (b + 1))
                            for gg2 in range(32):
                                b_group(b, gg2)

                    @blk.vector
                    def _(v):
                        v.wait_ge(s_yi, base["yi"] + 16 * NK)
                        for k in range(NK):
                            yk = Y_acc[:, k * PT:(k + 1) * PT]
                            v.tensor_tensor(t4a[:], yk, mrepA[:, t0p:t0p + PT], ALU.subtract)
                            v.drain()
                            v.tensor_tensor(ln2[:, k * PT:(k + 1) * PT], t4a[:], rrepA[:, t0p:t0p + PT],
                                            ALU.mult).then_inc(s_ln2, 1)
                            v.drain()
                        if p == 0:
                            v.wait_ge(s_gl, 32)
                            v.tensor_tensor(dgb[:], g0b, g1b, ALU.subtract)
                            v.drain()
                        v.engine_nop().then_inc(s_dg, 1)
                        for ns in range(NS4):
                            v.wait_ge(s_pd, base["pd"] + ns + 1)
                            sl = slice(ns * 512, (ns + 1) * 512)
                            v.tensor_scalar(d4[:, sl], yps[ns][:], 1.0, None,
                                            ALU.mult).then_inc(s_l4, 1)
                            v.wait_ge(s_r4, base["r4"] + ns + 1)
                            v.tensor_scalar(t4a[:, sl], t4a[:, sl], 1.0, None, ALU.add)
                            v.drain()
                            v.reciprocal(t4a[:, sl], t4a[:, sl])
                            v.drain()
                            v.tensor_single_scalar(d4[:, sl], d4[:, sl], 0.0, ALU.is_ge)
                            v.drain()
                            v.tensor_tensor(w0rep[:, sl], t4a[:, sl], d4[:, sl], ALU.mult)
                            v.drain()
                            v.tensor_tensor(w1rep[:, sl], t4a[:, sl], w0rep[:, sl], ALU.subtract)
                            v.drain()
                        def y_adds(bb):
                            for gg2 in range(32):
                                gl = 32 * bb + gg2
                                d, sx = gg2 // 2, gg2 % 2
                                v.wait_ge(s_mmB, base["mmB"] + gl + 1)
                                ya = Y_acc[:, d * PT + sx * 512: d * PT + (sx + 1) * 512]
                                v.tensor_tensor(ya, ya, yps[gl % 2][:], ALU.add).then_inc(s_evB, 1)
                        for b in range(NB4):
                            for jj in range(2 * RBAND):
                                j = 2 * RBAND * b + jj
                                fc, sx = j // 2, j % 2
                                e, f = chunks[fc]
                                v.wait_ge(s_evS, base["evS"] + j + 1)
                                tsl = tmp4[:, (j % 2) * 512:(j % 2 + 1) * 512]
                                v.tensor_tensor(tsl, tsl, ups[j % 2][:], ALU.mult)
                                v.drain()
                                hv = hb4[b % 2][:, (fc % RBAND) * PT + sx * 512:(fc % RBAND) * PT + (sx + 1) * 512]
                                v.tensor_tensor(hv, tsl, wrep[e][:, sx * 512:(sx + 1) * 512],
                                                ALU.mult).then_inc(s_evA, 1)
                                v.drain()
                            y_adds(b)

                    cnt["yi"] += 16 * NK
                    cnt["gl"] += 32
                    cnt["r4"] += NS4
                    cnt["l4"] += NS4
                    cnt["dg"] += 1
                    cnt["pd"] += NS4
                    cnt["ln2"] += NK
                    cnt["mmA"] += 2 * NE * NF
                    cnt["evS"] += 2 * NE * NF
                    cnt["evA"] += 2 * NE * NF
                    cnt["mmB"] += 32 * NB4
                    cnt["evB"] += 32 * NB4
                    cnt["w4"] = [cnt["w4"][i] + 32 * (NE * NF // 2) for i in range(2)]
                    cnt["wd4"] += 16 * NE * NF
                    cnt["out"] += 16 * NK

    return nc


def _layout_inputs(L0_attn_x, L0_attn_Wq, L0_attn_Wf, L0_attn_Wi, L0_attn_Wo,
                   L0_attn_g_weight, L0_ffn_gate_w, L0_ffn_w_gate, L0_ffn_w_up,
                   L0_ffn_w_down, T=2048):
    def kslab(W):
        return np.ascontiguousarray(np.asarray(W, np.float32).reshape(NK, 128, H))

    w_gate = np.asarray(L0_ffn_w_gate, np.float32)
    w_up = np.asarray(L0_ffn_w_up, np.float32)
    w_down = np.asarray(L0_ffn_w_down, np.float32)
    gate_w = np.asarray(L0_ffn_gate_w, np.float32)
    shared = {
        "wq": kslab(L0_attn_Wq),
        "wf": kslab(L0_attn_Wf),
        "wi": kslab(L0_attn_Wi),
        "wo": kslab(L0_attn_Wo),
        "wgt": np.ascontiguousarray(
            w_gate.reshape(NE, NF, 128, NK, 128).transpose(0, 1, 4, 3, 2)
        ).reshape(NE, NF, 128, H),
        "wut": np.ascontiguousarray(
            w_up.reshape(NE, NF, 128, NK, 128).transpose(0, 1, 4, 3, 2)
        ).reshape(NE, NF, 128, H),
        "wdt": np.ascontiguousarray(
            w_down.reshape(NE, NK, 128, NF, 128).transpose(0, 3, 4, 1, 2)
        ).reshape(NE, NF, 128, H),
        "grep": np.ascontiguousarray(
            np.broadcast_to(gate_w.reshape(NE, NK, 128, 1), (NE, NK, 128, 128))
        ).astype(np.float32),
        "gwt": np.ascontiguousarray(np.asarray(L0_attn_g_weight, np.float32).reshape(NH, 128).T),
        "cones": np.ones((128, 128), np.float32),
        "cident": np.eye(128, dtype=np.float32),
        "ctri": np.tile(np.triu(np.ones((128, 128), np.float32)), (1, 2)),
        "cbias": np.tile(np.array([EPS, 1.0, SH + LNC, -SH, SH, 0.0, 0.0, 0.0],
                                  np.float32), (128, 1)),
    }
    ncores = L0_attn_x.shape[0]
    in_maps = []
    for b in range(ncores):
        xf = np.ascontiguousarray(np.asarray(L0_attn_x[b], dtype=np.float32).T).reshape(NK, 128, T)
        in_maps.append(dict(shared, x_fm=xf))
    return in_maps


def kernel(L0_attn_x, L0_attn_Wq, L0_attn_Wf, L0_attn_Wi, L0_attn_Wo,
           L0_attn_g_weight, L0_ffn_gate_w, L0_ffn_w_gate, L0_ffn_w_up,
           L0_ffn_w_down):
    T = L0_attn_x.shape[1]
    in_maps = _layout_inputs(L0_attn_x, L0_attn_Wq, L0_attn_Wf, L0_attn_Wi,
                             L0_attn_Wo, L0_attn_g_weight, L0_ffn_gate_w,
                             L0_ffn_w_gate, L0_ffn_w_up, L0_ffn_w_down, T=T)
    nc = build_program(T=T)
    res = run_bass_kernel_spmd(nc, in_maps, core_ids=list(range(len(in_maps))))
    outs = []
    for r in res.results:
        o = np.asarray(r["out_s"], np.float32)
        outs.append(o.transpose(2, 0, 1).reshape(T, H))
    return np.stack(outs, axis=0)


# version 15
# speedup vs baseline: 1.1843x; 1.0033x over previous
import math
import numpy as np
from contextlib import ExitStack

import concourse.bass as bass
import concourse.mybir as mybir
from concourse.bass_utils import run_bass_kernel_spmd

F32 = mybir.dt.float32
BF16 = mybir.dt.bfloat16
AF = mybir.ActivationFunctionType
ALU = mybir.AluOpType

H = 2048
NK = 16
NH = 16
DK = 128
FF = 8192
NF = FF // 128
NE = 2
EPS = 1e-5
SH = 50.0
LNC = math.log(1.0 / math.sqrt(DK))
ACL = -135.0
RBAND = 8


def _preload_set(nc, eng, set_name):
    from concourse.hw_specs import get_activation_tables
    for idx, name in enumerate(get_activation_tables(nc.m.arch).keys()):
        if name == set_name:
            eng.add_instruction(mybir.InstLoadActFuncSet(
                name=nc.get_next_instruction_name(), act_func_set_id=idx, ins=[], outs=[]))
            return
    raise ValueError(set_name)


def build_program(T=2048, scratch_kind="Internal", dbg=False, stop_after=None):
    CH = T // 128
    NSL = T // 512
    PT = min(T, 1024)
    NP = T // PT
    NS4 = PT // 512
    assert NS4 == 2 and T % 1024 == 0

    nc = bass.Bass("TRN2", target_bir_lowering=False)

    def dram(name, shape, dt=F32, kind="ExternalInput"):
        return nc.dram_tensor(name, shape, dt, kind=kind).ap()

    x_fm = dram("x_fm", [NK, 128, T])
    wq = dram("wq", [NK, 128, H])
    wf = dram("wf", [NK, 128, H])
    wi = dram("wi", [NK, 128, H])
    wo = dram("wo", [NK, 128, H])
    wgt = dram("wgt", [NE, NF, 128, H])
    wut = dram("wut", [NE, NF, 128, H])
    wdt = dram("wdt", [NE, NF, 128, H])
    grep = dram("grep", [NE, NK, 128, 128])
    gwt = dram("gwt", [128, NH])
    cones = dram("cones", [128, 128])
    cident = dram("cident", [128, 128])
    ctri = dram("ctri", [128, 256])
    cbias = dram("cbias", [128, 8])

    q_s = dram("q_s", [T // 128, 128, H], BF16, kind=scratch_kind)
    g_s = dram("g_s", [T // 128, 128, H], F32, kind=scratch_kind)
    k_s = dram("k_s", [T // 128, 128, H], BF16, kind=scratch_kind)
    v_s = dram("v_s", [T // 128, 128, H], BF16, kind=scratch_kind)
    o_s = dram("o_s", [NH, 128, T], BF16, kind=scratch_kind)
    x1_s = dram("x1_s", [NK, 128, T], F32, kind=scratch_kind)
    out_s = dram("out_s", [NK, 128, T], F32, kind="ExternalOutput")
    if dbg:
        d_mrep = dram("d_mrep", [128, T], F32, kind="ExternalOutput")
        d_rrep = dram("d_rrep", [128, T], F32, kind="ExternalOutput")
        d_ln1 = dram("d_ln1", [128, NK * T], BF16, kind="ExternalOutput")

    TOP = ExitStack()
    with TOP:
        E = TOP.enter_context

        def sem(name):
            return E(nc.semaphore(name))

        onesb = E(nc.sbuf_tensor("onesb", [128, 128], BF16))
        onesf = E(nc.sbuf_tensor("onesf", [128, 128], F32))
        identb = E(nc.sbuf_tensor("identb", [128, 128], BF16))
        trif = E(nc.sbuf_tensor("trif", [128, 256], BF16))
        gwtile = E(nc.sbuf_tensor("gwtile", [128, NH], F32))
        cbt = E(nc.sbuf_tensor("cbt", [128, 8], F32))
        mrepA = E(nc.sbuf_tensor("mrepA", [128, T], F32))
        rrepA = E(nc.sbuf_tensor("rrepA", [128, T], F32))

        def bcol(i):
            return cbt[:, i:i + 1]

        s_bar = sem("s_bar")
        bar_n = [0]

        def barrier():
            with nc.Block() as blk:
                tgt = bar_n[0] + 5
                def mk(e):
                    e.drain()
                    e.sem_inc(s_bar, 1)
                    e.wait_ge(s_bar, tgt)
                blk.tensor(mk)
                blk.vector(mk)
                blk.scalar(mk)
                blk.gpsimd(mk)
                blk.sync(mk)
                bar_n[0] = tgt

        s_cst = sem("s_cst")
        s_cpl = sem("s_cpl")
        s_x0 = sem("s_x0")
        s_sq = sem("s_sq")
        s_mst = sem("s_mst")
        s_v0 = sem("s_v0")
        s_r0 = sem("s_r0")
        s_ln1 = sem("s_ln1")
        s_wh = [sem("s_wh0"), sem("s_wh1")]
        s_mmP = sem("s_mmP")
        s_evP = sem("s_evP")
        s_stP = [sem("s_stP0"), sem("s_stP1"), sem("s_stP2")]
        s_ev1 = sem("s_ev1")
        s_ev2 = sem("s_ev2")

        with ExitStack() as ph01:
            E01 = ph01.enter_context
            xb = E01(nc.sbuf_tensor("xb", [128, NK * T], BF16))
            mrep = E01(nc.sbuf_tensor("mrep", [128, T], BF16))
            rrep = E01(nc.sbuf_tensor("rrep", [128, T], BF16))
            wbuf = [E01(nc.sbuf_tensor(f"wbuf{i}", [128, NK * 1024], BF16)) for i in range(2)]

            with ExitStack() as ph:
                Ep = ph.enter_context
                xsq = [Ep(nc.sbuf_tensor(f"xsq{i}", [128, T], BF16)) for i in range(2)]
                t0a = Ep(nc.sbuf_tensor("t0a", [128, T], F32))
                t0b = Ep(nc.sbuf_tensor("t0b", [128, T], F32))
                ssum = [Ep(nc.psum_tensor(f"ssum{i}", [128, 512], F32)) for i in range(NSL)]
                ssq = [Ep(nc.psum_tensor(f"ssq{i}", [128, 512], F32)) for i in range(NSL)]
                blk = Ep(nc.Block())

                @blk.gpsimd
                def _(g):
                    g.dma_start(out=onesb[:], in_=cones).then_inc(s_cst, 16)
                    g.dma_start(out=identb[:], in_=cident).then_inc(s_cst, 16)
                    g.dma_start(out=trif[:], in_=ctri).then_inc(s_cst, 16)
                    for k in range(NK):
                        g.dma_start(out=xb[:, k * T:(k + 1) * T], in_=x_fm[k]).then_inc(s_x0, 16)
                    for k in range(NK):
                        g.dma_start(out=wbuf[0][:, k * 1024:(k + 1) * 1024],
                                    in_=wq[k][:, 0:1024]).then_inc(s_wh[0], 16)

                @blk.sync
                def _(sy):
                    sy.dma_start(out=onesf[:], in_=cones).then_inc(s_cpl, 16)
                    sy.dma_start(out=gwtile[:], in_=gwt).then_inc(s_cpl, 16)
                    sy.dma_start(out=cbt[:], in_=cbias).then_inc(s_cpl, 16)

                @blk.scalar
                def _(s_):
                    _preload_set(nc, s_, "natural_log_exp_and_others")
                    s_.wait_ge(s_x0, 16 * NK)
                    for k in range(NK):
                        if k >= 2:
                            s_.wait_ge(s_mst, k - 1)
                        s_.activation(xsq[k % 2][:], xb[:, k * T:(k + 1) * T], AF.Square).then_inc(s_sq, 1)
                    s_.wait_ge(s_cpl, 48)
                    for ns in range(NSL):
                        s_.wait_ge(s_v0, ns + 1)
                        sl = slice(ns * 512, (ns + 1) * 512)
                        s_.activation(t0a[:, sl], t0b[:, sl], AF.Ln, bias=bcol(0))
                        s_.drain()
                        s_.activation(rrep[:, sl], t0a[:, sl], AF.Exp, scale=-0.5).then_inc(s_r0, 1)

                @blk.tensor
                def _(t):
                    t.wait_ge(s_cst, 48)
                    t.wait_ge(s_x0, 16 * NK)
                    for k in range(NK):
                        t.wait_ge(s_sq, k + 1)
                        for ns in range(NSL):
                            sl = slice(ns * 512, (ns + 1) * 512)
                            t.matmul(ssum[ns][:], onesb[:], xb[:, k * T:(k + 1) * T][:, sl],
                                     start=(k == 0), stop=(k == NK - 1))
                            mm = t.matmul(ssq[ns][:], onesb[:], xsq[k % 2][:, sl],
                                          start=(k == 0), stop=(k == NK - 1))
                        mm.then_inc(s_mst, 1)

                @blk.vector
                def _(v):
                    for ns in range(NSL):
                        v.wait_ge(s_mst, NK)
                        sl = slice(ns * 512, (ns + 1) * 512)
                        v.tensor_scalar(mrep[:, sl], ssum[ns][:], 1.0 / H, None, ALU.mult)
                        v.drain()
                        v.tensor_tensor(t0a[:, sl], mrep[:, sl], mrep[:, sl], ALU.mult)
                        v.drain()
                        v.scalar_tensor_tensor(t0b[:, sl], ssq[ns][:], 1.0 / H, t0a[:, sl],
                                               ALU.mult, ALU.subtract).then_inc(s_v0, 1)

            NT = T // 512
            with ExitStack() as ph:
                Ep = ph.enter_context
                t1a = Ep(nc.sbuf_tensor("t1a", [128, T], F32))
                stq = [Ep(nc.sbuf_tensor(f"stq{i}", [128, 512], BF16)) for i in range(3)]
                stf = [Ep(nc.sbuf_tensor(f"stf{i}", [128, 512], F32)) for i in range(3)]
                e_ta = [Ep(nc.sbuf_tensor(f"e_ta{i}", [128, 512], F32)) for i in range(3)]
                e_tb = [Ep(nc.sbuf_tensor(f"e_tb{i}", [128, 512], F32)) for i in range(3)]
                e_tc = [Ep(nc.sbuf_tensor(f"e_tc{i}", [128, 512], F32)) for i in range(3)]
                e_td = Ep(nc.sbuf_tensor("e_td", [128, 512], F32))
                pp = [Ep(nc.psum_tensor(f"pp{i}", [128, 512], F32)) for i in range(6)]
                blk = Ep(nc.Block())

                groups = []
                for mat in range(3):
                    for hh in range(2):
                        hidx = mat * 2 + hh
                        if mat < 2:
                            for d8 in range(8):
                                for nt in range(NT):
                                    groups.append((mat, hidx, hh * 8 + d8, nt))
                        else:
                            for tb in range(T // 128):
                                for nt in range(2):
                                    groups.append((mat, hidx, tb, nt))
                wsrc = [wq, wf, wi]
                _fprefix = []
                _nf = 0
                for _g in groups:
                    if _g[0] == 1:
                        _nf += 1
                    _fprefix.append(_nf)

                def _fcount(gi):
                    return _fprefix[gi]
                gend = {}
                acc = 0
                for mat in range(3):
                    for hh in range(2):
                        hidx = mat * 2 + hh
                        acc += 8 * NT if mat < 2 else (T // 128) * 2
                        gend[hidx] = acc

                @blk.vector
                def _(v):
                    v.wait_ge(s_r0, NSL)
                    for k in range(NK):
                        xk = xb[:, k * T:(k + 1) * T]
                        v.tensor_tensor(t1a[:], xk, mrep[:], ALU.subtract)
                        v.drain()
                        v.tensor_tensor(xk, t1a[:], rrep[:], ALU.mult).then_inc(s_ln1, 1)
                        v.drain()
                    nf = 0
                    for gi, (mat, hidx, dd, nt) in enumerate(groups):
                        if mat != 1:
                            continue
                        nf += 1
                        v.wait_ge(s_ev1, nf)
                        v.tensor_tensor(e_td[:], pp[gi % 6][:], e_ta[gi % 3][:], ALU.subtract)
                        v.drain()
                        v.scalar_tensor_tensor(stf[gi % 3][:], e_td[:], 0.5, e_tc[gi % 3][:],
                                               ALU.mult, ALU.subtract).then_inc(s_ev2, 1)
                        v.drain()

                @blk.gpsimd
                def _(g):
                    for mat in range(3):
                        for hh in range(2):
                            hidx = mat * 2 + hh
                            if hidx == 0:
                                continue
                            if hidx >= 2:
                                g.wait_ge(s_mmP, gend[hidx - 2])
                            for k in range(NK):
                                g.dma_start(out=wbuf[hidx % 2][:, k * 1024:(k + 1) * 1024],
                                            in_=wsrc[mat][k][:, hh * 1024:(hh + 1) * 1024]
                                            ).then_inc(s_wh[hidx % 2], 16)

                @blk.tensor
                def _(t):
                    for gi, (mat, hidx, dd, nt) in enumerate(groups):
                        if gi == 2:
                            t.wait_ge(s_ln1, NK)
                        if gi == 0 or groups[gi - 1][1] != hidx:
                            t.wait_ge(s_wh[hidx % 2], 16 * NK * (hidx // 2 + 1))
                        if gi >= 6:
                            t.wait_ge(s_evP, gi - 5)
                        wb = wbuf[hidx % 2]
                        if mat < 2:
                            dlo = (dd % 8) * 128
                            for k in range(NK):
                                if gi < 2:
                                    t.wait_ge(s_ln1, k + 1)
                                mm = t.matmul(pp[gi % 6][:],
                                              wb[:, k * 1024 + dlo:k * 1024 + dlo + 128],
                                              xb[:, k * T + nt * 512: k * T + nt * 512 + 512],
                                              start=(k == 0), stop=(k == NK - 1))
                        else:
                            for k in range(NK):
                                mm = t.matmul(pp[gi % 6][:],
                                              xb[:, k * T + dd * 128: k * T + dd * 128 + 128],
                                              wb[:, k * 1024 + nt * 512: k * 1024 + nt * 512 + 512],
                                              start=(k == 0), stop=(k == NK - 1))
                        mm.then_inc(s_mmP, 1)

                cum = {0: [0] * len(groups), 1: [0] * len(groups), 2: [0] * len(groups)}
                tot = [0, 0, 0]
                for gi, (mat, hidx, dd, nt) in enumerate(groups):
                    tot[gi % 3] += 32 if mat == 1 else 16
                    for par in range(3):
                        cum[par][gi] = tot[par]

                def _cm(tensor_s, dd, nt):
                    return tensor_s[4 * nt:4 * nt + 4, :, dd * 128:(dd + 1) * 128].rearrange(
                        "c p t -> p c t")

                @blk.scalar
                def _(s_):
                    s_.preload_act_table(AF.Silu)
                    for gi, (mat, hidx, dd, nt) in enumerate(groups):
                        if gi > 0 and groups[gi - 1][0] == 0 and mat == 1:
                            _preload_set(nc, s_, "natural_log_exp_and_others")
                        s_.wait_ge(s_mmP, gi + 1)
                        if gi >= 3:
                            s_.wait_ge(s_stP[gi % 3], cum[gi % 3][gi - 3])
                        if mat == 0:
                            s_.activation(stq[gi % 3][:], pp[gi % 6][:], AF.Silu).then_inc(s_evP, 1)
                        elif mat == 1:
                            s_.activation(e_ta[gi % 3][:], pp[gi % 6][:], AF.Abs)
                            s_.drain()
                            s_.activation(e_tb[gi % 3][:], e_ta[gi % 3][:], AF.Exp, scale=-1.0)
                            s_.drain()
                            s_.activation(e_tc[gi % 3][:], e_tb[gi % 3][:], AF.Ln,
                                          bias=bcol(1)).then_inc(s_ev1, 1)
                            s_.wait_ge(s_ev2, _fcount(gi))
                            s_.activation(e_tb[gi % 3][:], stf[gi % 3][:], AF.Exp)
                            s_.drain()
                            s_.activation(stq[gi % 3][:], e_tb[gi % 3][:], AF.Copy,
                                          scale=-1.0, bias=1.0).then_inc(s_evP, 1)
                        else:
                            s_.copy(stq[gi % 3][:], pp[gi % 6][:]).then_inc(s_evP, 1)

                @blk.sync
                def _(sy):
                    if dbg:
                        sy.wait_ge(s_ln1, NK)
                        sy.dma_start(out=d_mrep, in_=mrep[:]).then_inc(s_cpl, 16)
                        sy.dma_start(out=d_rrep, in_=rrep[:]).then_inc(s_cpl, 16)
                        sy.dma_start(out=d_ln1, in_=xb[:]).then_inc(s_cpl, 16)
                        sy.wait_ge(s_cpl, 64 + 48)
                    nstores = [0, 0, 0]
                    for gi, (mat, hidx, dd, nt) in enumerate(groups):
                        sy.wait_ge(s_evP, gi + 1)
                        if mat == 0:
                            sy.dma_start(out=_cm(q_s, dd, nt),
                                         in_=stq[gi % 3][:]).then_inc(s_stP[gi % 3], 16)
                            nstores[gi % 3] += 16
                        elif mat == 1:
                            sy.dma_start(out=_cm(g_s, dd, nt),
                                         in_=stf[gi % 3][:]).then_inc(s_stP[gi % 3], 16)
                            sy.dma_start(out=_cm(k_s, dd, nt),
                                         in_=stq[gi % 3][:]).then_inc(s_stP[gi % 3], 16)
                            nstores[gi % 3] += 32
                        else:
                            hh2 = hidx - 4
                            dst = v_s[dd][:, hh2 * 1024 + nt * 512: hh2 * 1024 + (nt + 1) * 512]
                            sy.dma_start(out=dst, in_=stq[gi % 3][:]).then_inc(s_stP[gi % 3], 16)
                            nstores[gi % 3] += 16
                    sy.wait_ge(s_stP[0], nstores[0])
                    sy.wait_ge(s_stP[1], nstores[1])
                    sy.wait_ge(s_stP[2], nstores[2])

        barrier()
        if stop_after == 1:
            return nc

        s_sl = [sem("s_sl0"), sem("s_sl1")]
        s_slg = sem("s_slg")
        s_slk = sem("s_slk")
        s_slq = sem("s_slq")
        s_ae = sem("s_ae")
        s_ke = sem("s_ke")
        s_kd = sem("s_kd")
        s_vA = sem("s_vA")
        s_s1 = sem("s_s1")
        s_m1 = sem("s_m1")
        s_s2 = sem("s_s2")
        s_o1 = sem("s_o1")
        s_s3 = sem("s_s3")
        s_m3 = sem("s_m3")
        s_s4 = sem("s_s4")
        s_h4 = sem("s_h4")
        s_ost = sem("s_ost")
        s_pz = sem("s_pz")

        with ExitStack() as ph:
            Ep = ph.enter_context
            qsl = Ep(nc.sbuf_tensor("qsl", [128, 4 * H], BF16))
            vsl = [Ep(nc.sbuf_tensor(f"vsl{i}", [128, 4 * H], BF16)) for i in range(2)]
            gsl = Ep(nc.sbuf_tensor("gsl", [128, 4 * H], F32))
            ksl = Ep(nc.sbuf_tensor("ksl", [128, 4 * H], BF16))
            osl = Ep(nc.sbuf_tensor("osl", [128, NH * 512], BF16))
            hst = Ep(nc.sbuf_tensor("hst", [128, NH * 128], F32))
            hbf = Ep(nc.sbuf_tensor("hbf", [128, NH * 128], BF16))
            AA = [Ep(nc.sbuf_tensor(f"AA{i}", [128, NH * 128], F32)) for i in range(2)]
            e1t = Ep(nc.sbuf_tensor("e1t", [128, NH * 128], F32))
            e2t = Ep(nc.sbuf_tensor("e2t", [128, NH * 128], F32))
            qe = Ep(nc.sbuf_tensor("qe", [128, NH * 128], BF16))
            ke = Ep(nc.sbuf_tensor("ke", [128, NH * 128], BF16))
            kd = Ep(nc.sbuf_tensor("kd", [128, NH * 128], BF16))
            ssb = Ep(nc.sbuf_tensor("ssb", [128, NH * 128], BF16))
            ktm = Ep(nc.sbuf_tensor("ktm", [128, NH * 128], BF16))
            ec50 = Ep(nc.sbuf_tensor("ec50", [128, NH], F32))
            ecC = Ep(nc.sbuf_tensor("ecC", [128, NH], F32))
            sp = [Ep(nc.psum_tensor(f"sp{i}", [128, 256], F32)) for i in range(2)]
            op = [Ep(nc.psum_tensor(f"op{i}", [128, 256], F32)) for i in range(2)]
            tp = [Ep(nc.psum_tensor(f"tp{i}", [128, 512], BF16)) for i in range(2)]
            hp = [Ep(nc.psum_tensor(f"hp{i}", [128, 256], F32)) for i in range(2)]
            blk = Ep(nc.Block())

            def hsl(tile, hd, w=128):
                return tile[:, hd * w:(hd + 1) * w]

            def chv(tile, cc, hd):
                return tile[:, cc * H + hd * 128: cc * H + (hd + 1) * 128]

            @blk.sync
            def _(sy):
                def load_qv(s):
                    sy.dma_start(out=vsl[s % 2][:].rearrange("p (c t) -> p c t", c=4),
                                 in_=v_s[s * 4:(s + 1) * 4].rearrange("c p t -> p c t")
                                 ).then_inc(s_sl[s % 2], 16)

                def load_q(s):
                    sy.dma_start(out=qsl[:].rearrange("p (c t) -> p c t", c=4),
                                 in_=q_s[s * 4:(s + 1) * 4].rearrange("c p t -> p c t")
                                 ).then_inc(s_slq, 16)

                def load_g(s):
                    sy.dma_start(out=gsl[:].rearrange("p (c t) -> p c t", c=4),
                                 in_=g_s[s * 4:(s + 1) * 4].rearrange("c p t -> p c t")
                                 ).then_inc(s_slg, 16)

                def load_k(s):
                    sy.dma_start(out=ksl[:].rearrange("p (c t) -> p c t", c=4),
                                 in_=k_s[s * 4:(s + 1) * 4].rearrange("c p t -> p c t")
                                 ).then_inc(s_slk, 16)

                for s in range(min(2, NSL)):
                    load_qv(s)
                load_g(0)
                load_k(0)
                load_q(0)
                for s in range(NSL):
                    sy.wait_ge(s_o1, 32 * (s + 1))
                    t0 = s * 512
                    sy.dma_start(out=o_s[:, :, t0:t0 + 512].rearrange("h p t -> p h t"),
                                 in_=osl[:].rearrange("p (h t) -> p h t", h=NH)
                                 ).then_inc(s_ost, 16)
                    if s + 1 < NSL:
                        sy.wait_ge(s_vA, 4 * (s + 1))
                        load_g(s + 1)
                        sy.wait_ge(s_ke, 4 * (s + 1))
                        load_k(s + 1)
                        load_q(s + 1)
                    if s + 2 < NSL:
                        sy.wait_ge(s_h4, 64 * (s + 1))
                        load_qv(s + 2)
                sy.wait_ge(s_ost, 16 * NSL)

            @blk.scalar
            def _(s_):
                _preload_set(nc, s_, "natural_log_exp_and_others")
                for ci in range(CH):
                    s = ci // 4
                    cc = ci % 4
                    if ci >= 1:
                        s_.wait_ge(s_ke, ci)
                    s_.wait_ge(s_vA, ci + 1)
                    s_.activation(e2t[:], AA[ci % 2][:], AF.Exp, scale=-1.0, bias=bcol(3))
                    s_.drain()
                    s_.activation(e1t[:], AA[ci % 2][:], AF.Exp, scale=1.0, bias=bcol(2))
                    s_.drain()
                    aac = AA[ci % 2][:].rearrange("p (h t) -> p h t", h=NH)[:, :, 127:128]
                    s_.activation(ec50[:].rearrange("p (h o) -> p h o", o=1), aac,
                                  AF.Exp, bias=bcol(4)).then_inc(s_ae, 1)
                    if ci >= 1:
                        s_.wait_ge(s_s3, 4 * ci)
                    s_.wait_ge(s_ke, ci + 1)
                    for hd in range(NH):
                        s_.activation(hsl(kd, hd), hsl(ke, hd), AF.Copy,
                                      scale=ec50[:, hd:hd + 1])
                    s_.wait_ge(s_h4, 16 * ci)
                    s_.activation(hbf[:], hst[:], AF.Copy,
                                  scale=math.exp(-SH)).then_inc(s_kd, 1)
                    if s >= 1 and cc == 0:
                        s_.wait_ge(s_ost, 16 * s)
                    for pr in range(NH // 2):
                        s_.wait_ge(s_s2, 8 * ci + pr + 1)
                        dst = osl[:].rearrange("p (h t) -> p h t", h=NH)[
                            :, 2 * pr:2 * pr + 2, cc * 128:(cc + 1) * 128]
                        s_.copy(dst, op[pr % 2][:].rearrange("p (h t) -> p h t", h=2)
                                ).then_inc(s_o1, 1)

            @blk.vector
            def _(v):
                v.memset(hst[:], 0.0)
                v.memset(sp[0][:], 0.0)
                v.memset(sp[1][:], 0.0)
                v.drain()
                v.engine_nop().then_inc(s_pz, 1)
                for ci in range(CH):
                    s = ci // 4
                    cc = ci % 4
                    v.wait_ge(s_slg, 16 * (s + 1))
                    if ci >= 2:
                        v.wait_ge(s_ae, ci - 1)
                    for hd in range(NH):
                        v.tensor_tensor_scan(hsl(AA[ci % 2], hd), onesf[:], chv(gsl, cc, hd),
                                             0.0, ALU.mult, ALU.add)
                    v.drain()
                    v.tensor_single_scalar(AA[ci % 2][:], AA[ci % 2][:], ACL,
                                           ALU.max).then_inc(s_vA, 1)
                    v.wait_ge(s_slk, 16 * (s + 1))
                    v.wait_ge(s_slq, 16 * (s + 1))
                    if ci >= 1:
                        v.wait_ge(s_s2, 8 * ci)
                        v.wait_ge(s_kd, ci)
                    v.wait_ge(s_ae, ci + 1)
                    v.tensor_scalar(ecC[:], ec50[:], math.exp(-SH), None, ALU.mult)
                    v.tensor_tensor(ke[:], ksl[:, cc * H:(cc + 1) * H], e2t[:], ALU.mult)
                    v.tensor_tensor(qe[:], qsl[:, cc * H:(cc + 1) * H],
                                    e1t[:], ALU.mult).then_inc(s_ke, 1)
                    for pr in range(NH // 2):
                        v.wait_ge(s_s1, 8 * ci + pr + 1)
                        v.tensor_tensor(ssb[:, pr * 256:(pr + 1) * 256], sp[pr % 2][:], trif[:],
                                        ALU.mult).then_inc(s_m1, 1)
                    for qd in range(NH // 4):
                        v.wait_ge(s_s3, 4 * ci + qd + 1)
                        v.tensor_copy(ktm[:, qd * 512:(qd + 1) * 512],
                                      tp[qd % 2][:]).then_inc(s_m3, 1)
                    for hd in range(NH):
                        v.wait_ge(s_s4, 8 * ci + hd // 2 + 1)
                        v.scalar_tensor_tensor(hsl(hst, hd), hsl(hst, hd), ecC[:, hd:hd + 1],
                                               hp[(hd // 2) % 2][:, (hd % 2) * 128:(hd % 2 + 1) * 128],
                                               ALU.mult, ALU.add).then_inc(s_h4, 1)

            @blk.tensor
            def _(t):
                for ci in range(CH):
                    s = ci // 4
                    cc = ci % 4
                    t.wait_ge(s_ke, ci + 1)
                    t.wait_ge(s_sl[s % 2], 16 * (s // 2 + 1))
                    if ci == 0:
                        t.wait_ge(s_pz, 1)
                    for pr in range(NH // 2):
                        if 8 * ci + pr >= 2:
                            t.wait_ge(s_m1, 8 * ci + pr - 1)
                        for q2 in range(2):
                            hd = 2 * pr + q2
                            reg = sp[pr % 2][:, q2 * 128:(q2 + 1) * 128]
                            t.matmul(reg[0:64, :], hsl(ke, hd)[:, 0:64], hsl(qe, hd),
                                     start=True, stop=True)
                            mmq = t.matmul(reg[64:128, 64:128], hsl(ke, hd)[:, 64:128],
                                           hsl(qe, hd)[:, 64:128], start=True, stop=True)
                        mmq.then_inc(s_s1, 1)
                    t.wait_ge(s_kd, ci + 1)
                    for pr in range(NH // 2):
                        t.wait_ge(s_m1, 8 * ci + pr + 1)
                        if 8 * ci + pr >= 2:
                            t.wait_ge(s_o1, 8 * ci + pr - 1)
                        for q2 in range(2):
                            hd = 2 * pr + q2
                            reg = op[pr % 2][:, q2 * 128:(q2 + 1) * 128]
                            vv = chv(vsl[s % 2], cc, hd)
                            t.matmul(reg, vv, hsl(ssb, hd), start=True, stop=False)
                            mmq = t.matmul(reg, hsl(hbf, hd), hsl(qe, hd),
                                           start=False, stop=True)
                        mmq.then_inc(s_s2, 1)
                    for qd in range(NH // 4):
                        if 4 * ci + qd >= 2:
                            t.wait_ge(s_m3, 4 * ci + qd - 1)
                        for q4 in range(4):
                            hd = 4 * qd + q4
                            mmq = t.transpose(tp[qd % 2][:, q4 * 128:(q4 + 1) * 128],
                                              hsl(kd, hd), identb[:])
                        mmq.then_inc(s_s3, 1)
                    for pr in range(NH // 2):
                        t.wait_ge(s_m3, 4 * ci + pr // 2 + 1)
                        if 8 * ci + pr >= 2:
                            t.wait_ge(s_h4, 2 * (8 * ci + pr) - 2)
                        for q2 in range(2):
                            hd = 2 * pr + q2
                            vv = chv(vsl[s % 2], cc, hd)
                            mmq = t.matmul(hp[pr % 2][:, q2 * 128:(q2 + 1) * 128],
                                           hsl(ktm, hd), vv, start=True, stop=True)
                        mmq.then_inc(s_s4, 1)

        barrier()
        if stop_after == 2:
            return nc

        s_wo = sem("s_wo")
        s_sl3 = [sem("s_sl30"), sem("s_sl31")]
        s_slo3 = sem("s_slo3")
        s_sq3 = sem("s_sq3")
        s_st3 = sem("s_st3")
        s_rs3 = sem("s_rs3")
        s_oh3 = sem("s_oh3")
        s_mm3 = sem("s_mm3")
        s_ev3 = sem("s_ev3")
        s_x1st = [sem("s_x1st0"), sem("s_x1st1")]
        s_sq5 = sem("s_sq5")
        s_st4 = sem("s_st4")
        s_v5 = sem("s_v5")

        with ExitStack() as ph:
            Ep = ph.enter_context
            wob = Ep(nc.sbuf_tensor("wob", [128, NK * H], BF16))
            osl2s = Ep(nc.sbuf_tensor("osl2s", [128, NH * 512], BF16))
            xsl = [Ep(nc.sbuf_tensor(f"xsl{i}", [128, NK * 512], F32)) for i in range(2)]
            osq = Ep(nc.sbuf_tensor("osq", [128, NH * 512], BF16))
            ohat = Ep(nc.sbuf_tensor("ohat", [128, NH * 512], BF16))
            rstd3 = Ep(nc.sbuf_tensor("rstd3", [128, 512], F32))
            stg3 = [Ep(nc.sbuf_tensor(f"stg3{i}", [128, 512], F32)) for i in range(2)]
            xb3 = [Ep(nc.sbuf_tensor(f"xb3{i}", [128, 512], BF16)) for i in range(2)]
            xq3 = [Ep(nc.sbuf_tensor(f"xq3{i}", [128, 512], BF16)) for i in range(2)]
            t3b = Ep(nc.sbuf_tensor("t3b", [128, 512], F32))
            st3 = [Ep(nc.psum_tensor(f"st3{i}", [128, 512], F32)) for i in range(2)]
            pw = [Ep(nc.psum_tensor(f"pw{i}", [128, 512], F32)) for i in range(2)]
            st4s = Ep(nc.psum_tensor("st4s", [128, 512], F32))
            st4q = Ep(nc.psum_tensor("st4q", [128, 512], F32))
            blk = Ep(nc.Block())

            @blk.gpsimd
            def _(g):
                for k in range(NK):
                    g.dma_start(out=wob[:, k * H:(k + 1) * H], in_=wo[k]).then_inc(s_wo, 16)

            @blk.sync
            def _(sy):
                def load_o(s):
                    t0 = s * 512
                    sy.dma_start(out=osl2s[:].rearrange("p (h t) -> p h t", h=NH),
                                 in_=o_s[:, :, t0:t0 + 512].rearrange("h p t -> p h t")).then_inc(s_slo3, 16)
                def load_x(s):
                    t0 = s * 512
                    sy.dma_start(out=xsl[s % 2][:].rearrange("p (k t) -> p k t", k=NK),
                                 in_=x_fm[:, :, t0:t0 + 512].rearrange("k p t -> p k t")).then_inc(s_sl3[s % 2], 16)
                load_o(0)
                for s in range(min(2, NSL)):
                    load_x(s)
                nst = [0, 0]
                for s in range(NSL):
                    t0 = s * 512
                    if s + 1 < NSL:
                        sy.wait_ge(s_oh3, s + 1)
                        load_o(s + 1)
                    for d in range(NK):
                        gd = NK * s + d
                        sy.wait_ge(s_ev3, gd + 1)
                        sy.dma_start(out=x1_s[d][:, t0:t0 + 512],
                                     in_=stg3[gd % 2][:]).then_inc(s_x1st[gd % 2], 16)
                        nst[gd % 2] += 16
                    if s + 2 < NSL:
                        load_x(s + 2)
                sy.wait_ge(s_x1st[0], nst[0])
                sy.wait_ge(s_x1st[1], nst[1])

            @blk.scalar
            def _(s_):
                _preload_set(nc, s_, "natural_log_exp_and_others")
                for s in range(NSL):
                    s_.wait_ge(s_slo3, 16 * (s + 1))
                    s_.activation(osq[:], osl2s[:], AF.Square).then_inc(s_sq3, 1)
                    s_.wait_ge(s_st3, s + 1)
                    s_.activation(rstd3[:], st3[s % 2][:], AF.Ln, scale=1.0 / H, bias=bcol(0))
                    s_.drain()
                    s_.activation(rstd3[:], rstd3[:], AF.Exp, scale=-0.5).then_inc(s_rs3, 1)
                    for d in range(NK):
                        gd = NK * s + d
                        s_.wait_ge(s_ev3, gd + 1)
                        if gd >= 2:
                            s_.wait_ge(s_st4, gd - 1)
                        s_.copy(xb3[gd % 2][:], stg3[gd % 2][:])
                        s_.drain()
                        s_.activation(xq3[gd % 2][:], stg3[gd % 2][:], AF.Square).then_inc(s_sq5, 1)
                    s_.wait_ge(s_v5, s + 1)
                    sl5 = slice(s * 512, (s + 1) * 512)
                    s_.activation(t3b[:], t3b[:], AF.Ln, bias=bcol(0))
                    s_.drain()
                    s_.activation(rrepA[:, sl5], t3b[:], AF.Exp, scale=-0.5)

            @blk.vector
            def _(v):
                for s in range(NSL):
                    v.wait_ge(s_rs3, s + 1)
                    v.wait_ge(s_sl3[s % 2], 16 * (s // 2 + 1))
                    for hd in range(NH):
                        a = v.scalar_tensor_tensor(ohat[:, hd * 512:(hd + 1) * 512],
                                                   osl2s[:, hd * 512:(hd + 1) * 512],
                                                   gwtile[:, hd:hd + 1], rstd3[:],
                                                   ALU.mult, ALU.mult)
                    a.then_inc(s_oh3, 1)
                    for d in range(NK):
                        gd = NK * s + d
                        v.wait_ge(s_mm3, gd + 1)
                        if gd >= 2:
                            v.wait_ge(s_x1st[gd % 2], 16 * (gd // 2))
                            v.wait_ge(s_sq5, gd - 1)
                        v.tensor_tensor(stg3[gd % 2][:], pw[d % 2][:],
                                        xsl[s % 2][:, d * 512:(d + 1) * 512], ALU.add).then_inc(s_ev3, 1)
                    v.wait_ge(s_st4, NK * (s + 1))
                    sl5 = slice(s * 512, (s + 1) * 512)
                    v.tensor_scalar(mrepA[:, sl5], st4s[:], 1.0 / H, None, ALU.mult)
                    v.drain()
                    v.tensor_tensor(t3b[:], mrepA[:, sl5], mrepA[:, sl5], ALU.mult)
                    v.drain()
                    v.scalar_tensor_tensor(t3b[:], st4q[:], 1.0 / H, t3b[:],
                                           ALU.mult, ALU.subtract).then_inc(s_v5, 1)
                    v.drain()

            @blk.tensor
            def _(t):
                t.wait_ge(s_wo, 16 * NK)
                for s in range(NSL):
                    t.wait_ge(s_sq3, s + 1)
                    if s >= 2:
                        t.wait_ge(s_rs3, s - 1)
                        t.wait_ge(s_ev3, NK * (s - 1))
                    for hd in range(NH):
                        mm = t.matmul(st3[s % 2][:], onesb[:], osq[:, hd * 512:(hd + 1) * 512],
                                      start=(hd == 0), stop=(hd == NH - 1))
                    mm.then_inc(s_st3, 1)
                    t.wait_ge(s_oh3, s + 1)
                    for d in range(NK):
                        gd = NK * s + d
                        if gd >= 2:
                            t.wait_ge(s_ev3, gd - 1)
                        for k in range(NK):
                            mm = t.matmul(pw[d % 2][:], wob[:, k * H + d * 128: k * H + d * 128 + 128],
                                          ohat[:, k * 512:(k + 1) * 512],
                                          start=(k == 0), stop=(k == NK - 1))
                        mm.then_inc(s_mm3, 1)
                        if d >= 2:
                            gd2 = gd - 2
                            t.wait_ge(s_sq5, gd2 + 1)
                            t.matmul(st4s[:], onesb[:], xb3[gd2 % 2][:],
                                     start=(d == 2), stop=False)
                            mm = t.matmul(st4q[:], onesb[:], xq3[gd2 % 2][:],
                                          start=(d == 2), stop=False)
                            mm.then_inc(s_st4, 1)
                    for dd2 in range(NK - 2, NK):
                        gd2 = NK * s + dd2
                        t.wait_ge(s_sq5, gd2 + 1)
                        t.matmul(st4s[:], onesb[:], xb3[gd2 % 2][:],
                                 start=False, stop=(dd2 == NK - 1))
                        mm = t.matmul(st4q[:], onesb[:], xq3[gd2 % 2][:],
                                      start=False, stop=(dd2 == NK - 1))
                        mm.then_inc(s_st4, 1)

        barrier()
        if stop_after == 3:
            return nc

        s_yi = sem("s_yi")
        s_r4 = sem("s_r4")
        s_l4 = sem("s_l4")
        s_gl = sem("s_gl")
        s_dg = sem("s_dg")
        s_pd = sem("s_pd")
        s_ln2 = sem("s_ln2")
        s_w4 = [sem("s_w40"), sem("s_w41")]
        s_wd4 = sem("s_wd4")
        s_mmA = sem("s_mmA")
        s_evS = sem("s_evS")
        s_evA = sem("s_evA")
        s_mmB = sem("s_mmB")
        s_evB = sem("s_evB")
        s_out = sem("s_out")

        cnt = dict(yi=0, r4=0, l4=0, gl=0, dg=0, pd=0, ln2=0,
                   w4=[0, 0], wd4=0, mmA=0, evS=0, evA=0, mmB=0, evB=0, out=0)
        NB4 = NE * NF // RBAND

        with ExitStack() as ph4:
            E4 = ph4.enter_context
            Y_acc = E4(nc.sbuf_tensor("Y_acc", [128, NK * PT], F32))
            ln2 = E4(nc.sbuf_tensor("ln2", [128, NK * PT], BF16))
            w0rep = E4(nc.sbuf_tensor("w0rep", [128, PT], BF16))
            w1rep = E4(nc.sbuf_tensor("w1rep", [128, PT], BF16))
            hb4 = [E4(nc.sbuf_tensor(f"hb4_{i}", [128, RBAND * PT], BF16)) for i in range(2)]
            wgb = [E4(nc.sbuf_tensor(f"wgb{i}", [128, H], BF16)) for i in range(2)]
            wub = [E4(nc.sbuf_tensor(f"wub{i}", [128, H], BF16)) for i in range(2)]
            wdb = E4(nc.sbuf_tensor("wdb", [128, RBAND * H], BF16))
            tmp4 = E4(nc.sbuf_tensor("tmp4", [128, 2 * 512], BF16))
            t4a = E4(nc.sbuf_tensor("t4a", [128, PT], F32))
            dgb = E4(nc.sbuf_tensor("dgb", [128, NK * 128], BF16))
            g0b = hb4[0][:, 0:NK * 128]
            g1b = hb4[0][:, NK * 128:2 * NK * 128]
            gps = [E4(nc.psum_tensor(f"gps{i}", [128, 512], F32)) for i in range(2)]
            ups = [E4(nc.psum_tensor(f"ups{i}", [128, 512], F32)) for i in range(2)]
            yps = [E4(nc.psum_tensor(f"yps{i}", [128, 512], F32)) for i in range(2)]

            chunks = [(e, f) for e in range(NE) for f in range(NF)]
            wrep = [w0rep, w1rep]

            for p in range(NP):
                t0p = p * PT
                with ExitStack() as ph:
                    Ep = ph.enter_context
                    blk = Ep(nc.Block())
                    base = dict((k2, (list(v2) if isinstance(v2, list) else v2))
                                for k2, v2 in cnt.items())
                    d4 = rrepA[:, t0p:t0p + PT]

                    @blk.sync
                    def _(sy):
                        if p == 0:
                            for k in range(NK):
                                sy.dma_start(out=Y_acc[:, k * PT:(k + 1) * PT],
                                             in_=x1_s[k][:, t0p:t0p + PT]).then_inc(s_yi, 16)
                        for d in range(NK):
                            sy.wait_ge(s_evB, base["evB"] + 32 * (NB4 - 1) + 2 * d + 2)
                            sy.dma_start(out=out_s[d][:, t0p:t0p + PT],
                                         in_=Y_acc[:, d * PT:(d + 1) * PT]).then_inc(s_out, 16)
                            if p + 1 < NP:
                                sy.wait_ge(s_out, base["out"] + 16 * (d + 1))
                                sy.dma_start(out=Y_acc[:, d * PT:(d + 1) * PT],
                                             in_=x1_s[d][:, t0p + PT:t0p + 2 * PT]).then_inc(s_yi, 16)
                        sy.wait_ge(s_out, base["out"] + 16 * NK)

                    @blk.gpsimd
                    def _(g):
                        if p == 0:
                            g.dma_start(out=g0b.rearrange("p (k j) -> p k j", k=NK),
                                        in_=grep[0].rearrange("k p j -> p k j")).then_inc(s_gl, 16)
                            g.dma_start(out=g1b.rearrange("p (k j) -> p k j", k=NK),
                                        in_=grep[1].rearrange("k p j -> p k j")).then_inc(s_gl, 16)
                        for b in range(NB4):
                            for ii in range(RBAND):
                                fc = b * RBAND + ii
                                e, f = chunks[fc]
                                if fc >= 2:
                                    g.wait_ge(s_mmA, base["mmA"] + 2 * fc - 2)
                                g.dma_start(out=wgb[fc % 2][:], in_=wgt[e, f]).then_inc(s_w4[fc % 2], 16)
                                g.dma_start(out=wub[fc % 2][:], in_=wut[e, f]).then_inc(s_w4[fc % 2], 16)
                            if b >= 1:
                                g.wait_ge(s_mmB, base["mmB"] + 32 * b)
                            for ii in range(RBAND):
                                e, f = chunks[b * RBAND + ii]
                                g.dma_start(out=wdb[:, ii * H:(ii + 1) * H],
                                            in_=wdt[e, f]).then_inc(s_wd4, 16)


                    @blk.scalar
                    def _(s_):
                        _preload_set(nc, s_, "natural_log_exp_and_others")
                        for ns in range(NS4):
                            s_.wait_ge(s_l4, base["l4"] + ns + 1)
                            sl = slice(ns * 512, (ns + 1) * 512)
                            s_.activation(t4a[:, sl], d4[:, sl], AF.Abs)
                            s_.drain()
                            s_.activation(t4a[:, sl], t4a[:, sl], AF.Exp, scale=-1.0).then_inc(s_r4, 1)
                        _preload_set(nc, s_, "silu_and_others")
                        for j in range(2 * NE * NF):
                            s_.wait_ge(s_mmA, base["mmA"] + j + 1)
                            if j >= 2:
                                s_.wait_ge(s_evA, base["evA"] + j - 1)
                            s_.activation(tmp4[:, (j % 2) * 512:(j % 2 + 1) * 512], gps[j % 2][:],
                                          AF.Silu).then_inc(s_evS, 1)

                    @blk.tensor
                    def _(t):
                        def b_group(bb, gg2):
                            gl = 32 * bb + gg2
                            d, sx = gg2 // 2, gg2 % 2
                            if gl >= 2:
                                t.wait_ge(s_evB, base["evB"] + gl - 1)
                            for ii in range(RBAND):
                                mm = t.matmul(yps[gl % 2][:], wdb[:, ii * H + d * 128: ii * H + (d + 1) * 128],
                                              hb4[bb % 2][:, ii * PT + sx * 512: ii * PT + (sx + 1) * 512],
                                              start=(ii == 0), stop=(ii == RBAND - 1))
                            mm.then_inc(s_mmB, 1)
                        for b in range(NB4):
                            for jj in range(2 * RBAND):
                                j = 2 * RBAND * b + jj
                                fc, sx = j // 2, j % 2
                                if sx == 0:
                                    t.wait_ge(s_w4[fc % 2], base["w4"][fc % 2] + 32 * (fc // 2 + 1))
                                if j >= 2:
                                    t.wait_ge(s_evA, base["evA"] + j - 1)
                                    t.wait_ge(s_evS, base["evS"] + j - 1)
                                for k in range(NK):
                                    if j < 2:
                                        t.wait_ge(s_ln2, base["ln2"] + k + 1)
                                    t.matmul(gps[j % 2][:], wgb[fc % 2][:, k * 128:(k + 1) * 128],
                                             ln2[:, k * PT + sx * 512: k * PT + (sx + 1) * 512],
                                             start=(k == 0), stop=(k == NK - 1))
                                for k in range(NK):
                                    mm = t.matmul(ups[j % 2][:], wub[fc % 2][:, k * 128:(k + 1) * 128],
                                                  ln2[:, k * PT + sx * 512: k * PT + (sx + 1) * 512],
                                                  start=(k == 0), stop=(k == NK - 1))
                                mm.then_inc(s_mmA, 1)
                                if b == 0 and jj == 1:
                                    t.wait_ge(s_dg, base["dg"] + 1)
                                    t.wait_ge(s_ln2, base["ln2"] + NK)
                                    if p >= 1:
                                        t.wait_ge(s_evB, base["evB"])
                                    for ns in range(NS4):
                                        sl = slice(ns * 512, (ns + 1) * 512)
                                        for k in range(NK):
                                            mm = t.matmul(yps[ns][:], dgb[:][:, k * 128:(k + 1) * 128],
                                                          ln2[:, k * PT:(k + 1) * PT][:, sl],
                                                          start=(k == 0), stop=(k == NK - 1))
                                        mm.then_inc(s_pd, 1)
                            t.wait_ge(s_evA, base["evA"] + 2 * RBAND * (b + 1))
                            t.wait_ge(s_wd4, base["wd4"] + 16 * RBAND * (b + 1))
                            for gg2 in range(32):
                                b_group(b, gg2)

                    @blk.vector
                    def _(v):
                        v.wait_ge(s_yi, base["yi"] + 16 * NK)
                        for k in range(NK):
                            yk = Y_acc[:, k * PT:(k + 1) * PT]
                            v.tensor_tensor(t4a[:], yk, mrepA[:, t0p:t0p + PT], ALU.subtract)
                            v.drain()
                            v.tensor_tensor(ln2[:, k * PT:(k + 1) * PT], t4a[:], rrepA[:, t0p:t0p + PT],
                                            ALU.mult).then_inc(s_ln2, 1)
                            v.drain()
                        if p == 0:
                            v.wait_ge(s_gl, 32)
                            v.tensor_tensor(dgb[:], g0b, g1b, ALU.subtract)
                            v.drain()
                        v.engine_nop().then_inc(s_dg, 1)
                        for ns in range(NS4):
                            v.wait_ge(s_pd, base["pd"] + ns + 1)
                            sl = slice(ns * 512, (ns + 1) * 512)
                            v.tensor_scalar(d4[:, sl], yps[ns][:], 1.0, None,
                                            ALU.mult).then_inc(s_l4, 1)
                            v.wait_ge(s_r4, base["r4"] + ns + 1)
                            v.tensor_scalar(t4a[:, sl], t4a[:, sl], 1.0, None, ALU.add)
                            v.drain()
                            v.reciprocal(t4a[:, sl], t4a[:, sl])
                            v.drain()
                            v.tensor_single_scalar(d4[:, sl], d4[:, sl], 0.0, ALU.is_ge)
                            v.drain()
                            v.tensor_tensor(w0rep[:, sl], t4a[:, sl], d4[:, sl], ALU.mult)
                            v.drain()
                            v.tensor_tensor(w1rep[:, sl], t4a[:, sl], w0rep[:, sl], ALU.subtract)
                            v.drain()
                        def y_adds(bb):
                            for gg2 in range(32):
                                gl = 32 * bb + gg2
                                d, sx = gg2 // 2, gg2 % 2
                                v.wait_ge(s_mmB, base["mmB"] + gl + 1)
                                ya = Y_acc[:, d * PT + sx * 512: d * PT + (sx + 1) * 512]
                                v.tensor_tensor(ya, ya, yps[gl % 2][:], ALU.add).then_inc(s_evB, 1)
                        for b in range(NB4):
                            for jj in range(2 * RBAND):
                                j = 2 * RBAND * b + jj
                                fc, sx = j // 2, j % 2
                                e, f = chunks[fc]
                                v.wait_ge(s_evS, base["evS"] + j + 1)
                                tsl = tmp4[:, (j % 2) * 512:(j % 2 + 1) * 512]
                                v.tensor_tensor(tsl, tsl, ups[j % 2][:], ALU.mult)
                                v.drain()
                                hv = hb4[b % 2][:, (fc % RBAND) * PT + sx * 512:(fc % RBAND) * PT + (sx + 1) * 512]
                                v.tensor_tensor(hv, tsl, wrep[e][:, sx * 512:(sx + 1) * 512],
                                                ALU.mult).then_inc(s_evA, 1)
                                v.drain()
                            y_adds(b)

                    cnt["yi"] += 16 * NK
                    cnt["gl"] += 32
                    cnt["r4"] += NS4
                    cnt["l4"] += NS4
                    cnt["dg"] += 1
                    cnt["pd"] += NS4
                    cnt["ln2"] += NK
                    cnt["mmA"] += 2 * NE * NF
                    cnt["evS"] += 2 * NE * NF
                    cnt["evA"] += 2 * NE * NF
                    cnt["mmB"] += 32 * NB4
                    cnt["evB"] += 32 * NB4
                    cnt["w4"] = [cnt["w4"][i] + 32 * (NE * NF // 2) for i in range(2)]
                    cnt["wd4"] += 16 * NE * NF
                    cnt["out"] += 16 * NK

    return nc


def _layout_inputs(L0_attn_x, L0_attn_Wq, L0_attn_Wf, L0_attn_Wi, L0_attn_Wo,
                   L0_attn_g_weight, L0_ffn_gate_w, L0_ffn_w_gate, L0_ffn_w_up,
                   L0_ffn_w_down, T=2048):
    def kslab(W):
        return np.ascontiguousarray(np.asarray(W, np.float32).reshape(NK, 128, H))

    w_gate = np.asarray(L0_ffn_w_gate, np.float32)
    w_up = np.asarray(L0_ffn_w_up, np.float32)
    w_down = np.asarray(L0_ffn_w_down, np.float32)
    gate_w = np.asarray(L0_ffn_gate_w, np.float32)
    shared = {
        "wq": kslab(L0_attn_Wq),
        "wf": kslab(L0_attn_Wf),
        "wi": kslab(L0_attn_Wi),
        "wo": kslab(L0_attn_Wo),
        "wgt": np.ascontiguousarray(
            w_gate.reshape(NE, NF, 128, NK, 128).transpose(0, 1, 4, 3, 2)
        ).reshape(NE, NF, 128, H),
        "wut": np.ascontiguousarray(
            w_up.reshape(NE, NF, 128, NK, 128).transpose(0, 1, 4, 3, 2)
        ).reshape(NE, NF, 128, H),
        "wdt": np.ascontiguousarray(
            w_down.reshape(NE, NK, 128, NF, 128).transpose(0, 3, 4, 1, 2)
        ).reshape(NE, NF, 128, H),
        "grep": np.ascontiguousarray(
            np.broadcast_to(gate_w.reshape(NE, NK, 128, 1), (NE, NK, 128, 128))
        ).astype(np.float32),
        "gwt": np.ascontiguousarray(np.asarray(L0_attn_g_weight, np.float32).reshape(NH, 128).T),
        "cones": np.ones((128, 128), np.float32),
        "cident": np.eye(128, dtype=np.float32),
        "ctri": np.tile(np.triu(np.ones((128, 128), np.float32)), (1, 2)),
        "cbias": np.tile(np.array([EPS, 1.0, SH + LNC, -SH, SH, 0.0, 0.0, 0.0],
                                  np.float32), (128, 1)),
    }
    ncores = L0_attn_x.shape[0]
    in_maps = []
    for b in range(ncores):
        xf = np.ascontiguousarray(np.asarray(L0_attn_x[b], dtype=np.float32).T).reshape(NK, 128, T)
        in_maps.append(dict(shared, x_fm=xf))
    return in_maps


def kernel(L0_attn_x, L0_attn_Wq, L0_attn_Wf, L0_attn_Wi, L0_attn_Wo,
           L0_attn_g_weight, L0_ffn_gate_w, L0_ffn_w_gate, L0_ffn_w_up,
           L0_ffn_w_down):
    T = L0_attn_x.shape[1]
    in_maps = _layout_inputs(L0_attn_x, L0_attn_Wq, L0_attn_Wf, L0_attn_Wi,
                             L0_attn_Wo, L0_attn_g_weight, L0_ffn_gate_w,
                             L0_ffn_w_gate, L0_ffn_w_up, L0_ffn_w_down, T=T)
    nc = build_program(T=T)
    res = run_bass_kernel_spmd(nc, in_maps, core_ids=list(range(len(in_maps))))
    outs = []
    for r in res.results:
        o = np.asarray(r["out_s"], np.float32)
        outs.append(o.transpose(2, 0, 1).reshape(T, H))
    return np.stack(outs, axis=0)
